# version 07 (final)
# baseline (speedup 1.0000x reference)
import sys
if "/opt/trn_rl_repo" not in sys.path:
    sys.path.insert(0, "/opt/trn_rl_repo")
import numpy as np
import concourse.bass as bass
import concourse.mybir as mybir
from concourse.bass_utils import run_bass_kernel_spmd

F32 = mybir.dt.float32
BF16 = mybir.dt.bfloat16
AF = mybir.ActivationFunctionType
ALU = mybir.AluOpType
AX = mybir.AxisListType


class Cfg:
    def __init__(self, KC, JC, S, NSP):
        self.KC = KC
        self.JC = JC
        self.JB = 2 * JC
        self.OB = KC
        self.NE = 8
        self.S = S
        self.NSP = NSP
        self.T = S * NSP
        self.D = 128 * KC
        self.F = 128 * JC
        self.NT = min(512, S)
        self.GX = min(4, KC)
        self.NGX = KC // self.GX
        self.NTL = S // self.NT
        self.TT = S // 128


CFG_REAL = Cfg(KC=16, JC=11, S=1024, NSP=4)

C_EPS = 0
C_NHALF = 1
C_RD = 2
C_RT = 3
C_ONE = 4


class Op:
    __slots__ = ("stream", "engine", "fn", "idx", "waits", "tag", "no_inc")

    def __init__(self, stream, engine, fn, idx, tag):
        self.stream = stream
        self.engine = engine
        self.fn = fn
        self.idx = idx
        self.waits = []
        self.tag = tag
        self.no_inc = False


class Sched:
    ENGINES = ("t", "v", "s", "gx", "sx")
    DMA_ENG = frozenset(("gx", "sx"))

    def __init__(self):
        self.ops = {}
        self.eng_ops = {e: [] for e in self.ENGINES}
        self.order = []
        self.last_writer = {}
        self.readers = {}
        self.vc = {}
        self.engine_vc = {e: {} for e in self.ENGINES}
        self.snap = {}

    def is_dma(self, stream):
        return stream.split(":")[0] in self.DMA_ENG

    def op(self, stream, fn, reads=(), writes=(), tag=""):
        engine = stream.split(":")[0]
        lst = self.ops.setdefault(stream, [])
        o = Op(stream, engine, fn, len(lst), tag)
        lst.append(o)
        self.eng_ops[engine].append(o)
        self.order.append(o)
        deps = set()
        for b in reads:
            w = self.last_writer.get(b)
            if w is not None:
                deps.add(w)
            self.readers.setdefault(b, []).append(o)
        for b in writes:
            w = self.last_writer.get(b)
            if w is not None:
                deps.add(w)
            for r in self.readers.get(b, ()):
                deps.add(r)
            self.readers[b] = []
            self.last_writer[b] = o
        deps.discard(o)
        V = self.engine_vc[engine]
        for d in sorted(deps, key=lambda d: (d.stream, d.idx)):
            if d.engine == "t" and engine == "t":
                continue
            if d.engine == engine and d.stream == stream and not self.is_dma(stream):
                pass
            need = d.idx + 1
            if V.get(d.stream, 0) >= need:
                continue
            o.waits.append((d.stream, need))
            for k, v in self.snap[d].items():
                if V.get(k, 0) < v:
                    V[k] = v
            if V.get(d.stream, 0) < need:
                V[d.stream] = need
        s = dict(V)
        s[stream] = o.idx + 1
        self.snap[o] = s
        return o

    def finalize(self):
        for o in self.order:
            for (ws, wc) in o.waits:
                assert wc <= len(self.ops[ws]), (o.tag, ws, wc)
        return {st: len(v) for st, v in self.ops.items()}


def build_program(cfg):
    KC, JC, JB, OB, NE = cfg.KC, cfg.JC, cfg.JB, cfg.OB, cfg.NE
    S, NSP, T, D = cfg.S, cfg.NSP, cfg.T, cfg.D
    NT, NTL, TT = cfg.NT, cfg.NTL, cfg.TT
    EXOFF = max(S, 512)

    nc = bass.Bass("TRN2", target_bir_lowering=False)
    xt_d = nc.dram_tensor("xt", [D, T], F32, kind="ExternalInput").ap()
    w1_d = nc.dram_tensor("w1r", [NE, JB, 128, KC * 128], F32, kind="ExternalInput").ap()
    w2_d = nc.dram_tensor("w2r", [NE, OB, 128, JC * 128], F32, kind="ExternalInput").ap()
    wgt_d = nc.dram_tensor("wgt", [128, KC * 8], F32, kind="ExternalInput").ap()
    wvt_d = nc.dram_tensor("wvt", [OB, 128, KC * 128], F32, kind="ExternalInput").ap()
    wot_d = nc.dram_tensor("wot", [OB, 128, KC * 128], F32, kind="ExternalInput").ap()
    bvt_d = nc.dram_tensor("bvt", [128, KC], F32, kind="ExternalInput").ap()
    bot_d = nc.dram_tensor("bot", [128, KC], F32, kind="ExternalInput").ap()
    sel_d = nc.dram_tensor("sel", [8, NE * 128], F32, kind="ExternalInput").ap()
    idn_d = nc.dram_tensor("idn", [128, 128], F32, kind="ExternalInput").ap()
    cst_d = nc.dram_tensor("cst", [128, 8], F32, kind="ExternalInput").ap()
    onesr_d = nc.dram_tensor("onesr", [1, 128], F32, kind="ExternalInput").ap()
    onesk_d = nc.dram_tensor("onesk", [128, 1], F32, kind="ExternalInput").ap()
    out_d = nc.dram_tensor("out", [D, T], F32, kind="ExternalOutput").ap()
    xf_d = nc.dram_tensor("xf", [D, T], F32, kind="Internal").ap()
    w1c_d = nc.dram_tensor("w1c", [NE, JB, 128, KC * 128], BF16, kind="Internal").ap()
    w2c_d = nc.dram_tensor("w2c", [NE, OB, 128, JC * 128], BF16, kind="Internal").ap()

    xt_r = xt_d.rearrange("(c p) t -> p c t", p=128)
    xf_r = xf_d.rearrange("(c p) t -> p c t", p=128)
    out_r = out_d.rearrange("(c p) t -> p c t", p=128)

    from contextlib import ExitStack
    with ExitStack() as ctx:
        E_ = ctx.enter_context
        X = E_(nc.sbuf_tensor("X", [128, KC * S], F32))
        XLN = E_(nc.sbuf_tensor("XLN", [128, KC * S], BF16))
        G = E_(nc.sbuf_tensor("G", [128, JC * S], BF16))
        SQ = E_(nc.sbuf_tensor("SQ", [128, 4 * S], BF16))
        SILU = E_(nc.sbuf_tensor("SILU", [128, S], F32))
        TMP = E_(nc.sbuf_tensor("TMP", [128, S], F32))
        CH = E_(nc.sbuf_tensor("CH", [128, S], F32))
        W1B = E_(nc.sbuf_tensor("W1B", [128, 2 * KC * 128], BF16))
        W2B = E_(nc.sbuf_tensor("W2B", [128, 2 * JC * 128], BF16))
        LG = E_(nc.sbuf_tensor("LG", [8, S], F32))
        CS = E_(nc.sbuf_tensor("CS", [8, S], F32))
        R_MU = E_(nc.sbuf_tensor("R_MU", [1, S], F32))
        R_V = E_(nc.sbuf_tensor("R_V", [1, S], F32))
        R_MR = E_(nc.sbuf_tensor("R_MR", [1, S], F32))
        R_T4 = E_(nc.sbuf_tensor("R_T4", [1, S], F32))
        RT = E_(nc.sbuf_tensor("RT", [128, 8 * TT], F32))
        DD = E_(nc.sbuf_tensor("DD", [128, 8 * TT], F32))
        SELS = E_(nc.sbuf_tensor("SELS", [128, 8 * TT], F32))
        EE = E_(nc.sbuf_tensor("EE", [128, 8 * TT], F32))
        CTT = E_(nc.sbuf_tensor("CTT", [128, 8 * TT], F32))
        MRB = E_(nc.sbuf_tensor("MRB", [128, 8], F32))
        COLS = E_(nc.sbuf_tensor("COLS", [128, 4], F32))
        DL2S = E_(nc.sbuf_tensor("DL2S", [128, TT], F32))
        E2 = E_(nc.sbuf_tensor("E2", [128, TT], F32))
        DEN = E_(nc.sbuf_tensor("DEN", [128, TT], F32))
        RDEN = E_(nc.sbuf_tensor("RDEN", [128, TT], F32))
        A1 = E_(nc.sbuf_tensor("A1", [128, KC], F32))
        A1SP = E_(nc.sbuf_tensor("A1SP", [128, NSP * KC], F32))
        SMR = E_(nc.sbuf_tensor("SMR", [1, 2], F32))
        SMRC = E_(nc.sbuf_tensor("SMRC", [128, 1], F32))
        XBAR = E_(nc.sbuf_tensor("XBAR", [128, KC], F32))
        VC = E_(nc.sbuf_tensor("VC", [128, KC], F32))
        AC = E_(nc.sbuf_tensor("AC", [128, KC], F32))
        BVT = E_(nc.sbuf_tensor("BVT", [128, KC], F32))
        BOT = E_(nc.sbuf_tensor("BOT", [128, KC], F32))
        SLAB = E_(nc.sbuf_tensor("SLAB", [128, 2 * KC * 128], F32))
        CST = E_(nc.sbuf_tensor("CST", [128, 8], F32))
        IDN = E_(nc.sbuf_tensor("IDN", [128, 128], F32))
        SEL = E_(nc.sbuf_tensor("SEL", [8, NE * 128], F32))
        WGTB = E_(nc.sbuf_tensor("WGTB", [128, KC * 8], BF16))
        ONESR = E_(nc.sbuf_tensor("ONESR", [1, 128], F32))
        ONESK = E_(nc.sbuf_tensor("ONESK", [128, 1], F32))
        ONESKB = E_(nc.sbuf_tensor("ONESKB", [128, 1], BF16))
        pA = E_(nc.psum_tensor("pA", [128, max(2 * S, EXOFF + max(S, 512))], F32))
        pB = E_(nc.psum_tensor("pB", [128, 2 * S], F32))

        sched = Sched()

        def pa_keys(lo, hi):
            return [f"pA.{b}" for b in range(lo // 512, (hi - 1) // 512 + 1)]

        def pb_keys(lo, hi):
            return [f"pB.{b}" for b in range(lo // 512, (hi - 1) // 512 + 1)]

        def const_loads():
            pairs = [(CST, cst_d, "CST"), (IDN, idn_d, "IDN"), (SEL, sel_d, "SEL"),
                     (ONESR, onesr_d, "ONESR"),
                     (ONESK, onesk_d, "ONESK"), (BVT, bvt_d, "BVT"), (BOT, bot_d, "BOT")]
            for tb, src, key in pairs:
                sched.op(f"sx:c.{key}", lambda e, tb=tb, src=src: e.dma_start(out=tb[:, :], in_=src),
                         writes=[key], tag=f"ld_{key}")
            sched.op("gx:c.WGTB", lambda e: e.dma_start(out=WGTB[:, :], in_=wgt_d),
                     writes=["WGTB"], tag="ld_WGTB")
            sched.op("gx:c.ONESKB", lambda e: e.dma_start(out=ONESKB[:, :], in_=onesk_d),
                     writes=["ONESKB"], tag="ld_ONESKB")
            sched.op("v", lambda e: e.memset(SMR[:, :], 0.0), writes=["SMR"], tag="z_SMR")

        GXC, NGX = cfg.GX, cfg.NGX

        def load_x(sp, src_r, srckey):
            for g in range(NGX):
                def ld(e, g=g, sp=sp, src_r=src_r):
                    return e.dma_start(
                        out=X.ap().rearrange("p (c t) -> p c t", t=S)[:, GXC * g:GXC * (g + 1), :],
                        in_=src_r[:, GXC * g:GXC * (g + 1), sp * S:(sp + 1) * S])
                rd = [] if srckey is None else [f"{srckey}.{sp}.{g}"]
                sched.op(f"sx:x{g}", ld, reads=rd,
                         writes=[f"X.{c}" for c in range(GXC * g, GXC * (g + 1))], tag=f"ldx{sp}.{g}")

        def add_ac(sp):
            for c in range(KC):
                sched.op("v", lambda e, c=c: e.tensor_scalar(
                    X[:, c * S:(c + 1) * S], X[:, c * S:(c + 1) * S],
                    AC[:, c:c + 1], None, ALU.add),
                    reads=[f"X.{c}", "AC"], writes=[f"X.{c}"], tag=f"addac{c}")

        def stats_sq(tag, c):
            p = c % 4
            sched.op("v", lambda e, c=c, p=p: e.tensor_tensor(
                SQ[:, p * S:(p + 1) * S], X[:, c * S:(c + 1) * S],
                X[:, c * S:(c + 1) * S], ALU.mult),
                reads=[f"X.{c}"], writes=[f"SQ.{p}"], tag=f"{tag}_sq{c}")

        def stats_xsum(tag, c):
            def smm(e, c=c):
                for n in range(NTL):
                    i = e.matmul(pA[0:1, n * NT:(n + 1) * NT], ONESK[:, :],
                                 X[:, c * S + n * NT:c * S + (n + 1) * NT],
                                 start=(c == 0), stop=(c == KC - 1))
                return i
            sched.op("t", smm, reads=[f"X.{c}", "ONESK"],
                     writes=pa_keys(0, S), tag=f"{tag}_xmm{c}")

        def stats_ex2(tag, c):
            p = c % 4
            def smm(e, c=c, p=p):
                for n in range(NTL):
                    i = e.matmul(pA[0:1, EXOFF + n * NT:EXOFF + (n + 1) * NT], ONESKB[:, :],
                                 SQ[:, p * S + n * NT:p * S + (n + 1) * NT],
                                 start=(c == 0), stop=(c == KC - 1))
                return i
            sched.op("t", smm, reads=[f"SQ.{p}", "ONESKB"],
                     writes=pa_keys(EXOFF, EXOFF + S), tag=f"{tag}_emm{c}")

        def stats_rows(tag, mr):
            sched.op("v", lambda e: e.tensor_scalar(R_MU[:, :], pA[0:1, 0:S],
                     CST[0:1, C_RD:C_RD + 1], None, ALU.mult),
                     reads=pa_keys(0, S) + ["CST"], writes=["R_MU"], tag=f"{tag}_rmu")
            sched.op("v", lambda e: e.tensor_tensor(R_T4[:, :], R_MU[:, :], R_MU[:, :], ALU.mult),
                     reads=["R_MU"], writes=["R_T4"], tag=f"{tag}_rmu2")
            sched.op("v", lambda e: e.scalar_tensor_tensor(
                R_V[:, :], pA[0:1, EXOFF:EXOFF + S], CST[0:1, C_RD:C_RD + 1], R_T4[:, :],
                ALU.mult, ALU.subtract),
                reads=pa_keys(EXOFF, EXOFF + S) + ["CST", "R_T4"], writes=["R_V"], tag=f"{tag}_rvar")
            sched.op("s", lambda e: e.activation(R_V[:, :], R_V[:, :], AF.Ln,
                     bias=CST[0:1, C_EPS:C_EPS + 1]),
                     reads=["R_V", "CST"], writes=["R_V"], tag=f"{tag}_rln")
            sched.op("s", lambda e: e.activation(R_V[:, :], R_V[:, :], AF.Exp,
                     scale=CST[0:1, C_NHALF:C_NHALF + 1]),
                     reads=["R_V", "CST"], writes=["R_V"], tag=f"{tag}_rexp")
            if mr:
                sched.op("v", lambda e: e.tensor_tensor(R_MR[:, :], R_MU[:, :], R_V[:, :], ALU.mult),
                         reads=["R_MU", "R_V"], writes=["R_MR"], tag=f"{tag}_rmr")

        def stats(tag, mr=False):
            for c in range(KC):
                stats_sq(tag, c)
                stats_xsum(tag, c)
                stats_ex2(tag, c)
            stats_rows(tag, mr)

        def ln_and_router(sp):
            def bc(e):
                for n in range(NTL):
                    e.matmul(pB[:, n * NT:(n + 1) * NT], ONESR[:, :],
                             R_MU[:, n * NT:(n + 1) * NT], start=True, stop=True)
                    i = e.matmul(pB[:, S + n * NT:S + (n + 1) * NT], ONESR[:, :],
                                 R_V[:, n * NT:(n + 1) * NT], start=True, stop=True)
                return i
            sched.op("t", bc, reads=["R_MU", "R_V", "ONESR"], writes=pb_keys(0, 2 * S), tag="lnbc")
            for c in range(KC):
                sched.op("v", lambda e, c=c: e.tensor_tensor(
                    TMP[:, :], X[:, c * S:(c + 1) * S], pB[:, 0:S], ALU.subtract),
                    reads=[f"X.{c}"] + pb_keys(0, S), writes=["TMP"], tag=f"lna{c}")
                sched.op("v", lambda e, c=c: e.tensor_tensor(
                    XLN[:, c * S:(c + 1) * S], TMP[:, :], pB[:, S:2 * S], ALU.mult),
                    reads=["TMP"] + pb_keys(S, 2 * S), writes=[f"XLN.{c}"], tag=f"lnb{c}")
            for c in range(KC):
                def rmm(e, c=c):
                    for n in range(NTL):
                        i = e.matmul(pA[0:8, n * NT:(n + 1) * NT], WGTB[:, c * 8:(c + 1) * 8],
                                     XLN[:, c * S + n * NT:c * S + (n + 1) * NT],
                                     start=(c == 0), stop=(c == KC - 1))
                    return i
                sched.op("t", rmm, reads=[f"XLN.{c}", "WGTB"], writes=pa_keys(0, S), tag=f"rmm{c}")
            sched.op("s", lambda e: e.copy(LG[:, :], pA[0:8, 0:S]),
                     reads=pa_keys(0, S), writes=["LG"], tag="lgcp")
            def tp(e):
                for t_ in range(TT):
                    i = e.transpose(pA[:, S + 8 * t_:S + 8 * t_ + 8],
                                    LG[0:8, t_ * 128:(t_ + 1) * 128], IDN[0:8, 0:8])
                return i
            sched.op("t", tp, reads=["LG", "IDN"], writes=pa_keys(S, S + 8 * TT), tag="lgtp")
            sched.op("v", lambda e: e.tensor_copy(RT[:, :], pA[:, S:S + 8 * TT]),
                     reads=pa_keys(S, S + 8 * TT), writes=["RT"], tag="rtcp")
            for t_ in range(TT):
                rt = RT[:, 8 * t_:8 * t_ + 8]
                sched.op("v", lambda e, rt=rt: e.max(out=MRB[:, :], in_=rt),
                         reads=["RT"], writes=["MRB"], tag=f"m8_{t_}")
                sched.op("v", lambda e, rt=rt, t_=t_: e.tensor_scalar(
                    DD[:, 8 * t_:8 * t_ + 8], rt, MRB[:, 0:1], None, ALU.subtract),
                    reads=["RT", "MRB"], writes=["DD"], tag=f"dd_{t_}")
                sched.op("v", lambda e, t_=t_: e.tensor_tensor(
                    DL2S[:, t_:t_ + 1], MRB[:, 1:2], MRB[:, 0:1], ALU.subtract),
                    reads=["MRB"], writes=["DL2S"], tag=f"dl2_{t_}")
                sched.op("v", lambda e, t_=t_: e.tensor_scalar(
                    SELS[:, 8 * t_:8 * t_ + 8], DD[:, 8 * t_:8 * t_ + 8],
                    DL2S[:, t_:t_ + 1], None, ALU.is_ge),
                    reads=["DD", "DL2S"], writes=["SELS"], tag=f"sel_{t_}")
            sched.op("s", lambda e: e.activation(EE[:, :], DD[:, :], AF.Exp),
                     reads=["DD"], writes=["EE"], tag="ee")
            sched.op("s", lambda e: e.activation(E2[:, :], DL2S[:, :], AF.Exp),
                     reads=["DL2S"], writes=["E2"], tag="e2")
            sched.op("v", lambda e: e.tensor_scalar(DEN[:, :], E2[:, :],
                     CST[:, C_ONE:C_ONE + 1], None, ALU.add),
                     reads=["E2", "CST"], writes=["DEN"], tag="den")
            sched.op("v", lambda e: e.reciprocal(RDEN[:, :], DEN[:, :]),
                     reads=["DEN"], writes=["RDEN"], tag="rden")
            for t_ in range(TT):
                sched.op("v", lambda e, t_=t_: e.tensor_scalar(
                    CTT[:, 8 * t_:8 * t_ + 8], EE[:, 8 * t_:8 * t_ + 8],
                    RDEN[:, t_:t_ + 1], None, ALU.mult),
                    reads=["EE", "RDEN"], writes=["CTT"], tag=f"ctw_{t_}")
                sched.op("v", lambda e, t_=t_: e.tensor_tensor(
                    CTT[:, 8 * t_:8 * t_ + 8], CTT[:, 8 * t_:8 * t_ + 8],
                    SELS[:, 8 * t_:8 * t_ + 8], ALU.mult),
                    reads=["CTT", "SELS"], writes=["CTT"], tag=f"cts_{t_}")
            def tpb(e):
                for t_ in range(TT):
                    i = e.transpose(pA[0:8, t_ * 128:(t_ + 1) * 128],
                                    CTT[:, 8 * t_:8 * t_ + 8], IDN[:, :])
                return i
            sched.op("t", tpb, reads=["CTT", "IDN"], writes=pa_keys(0, S), tag="cttp")
            sched.op("s", lambda e: e.copy(CS[:, :], pA[0:8, 0:S]),
                     reads=pa_keys(0, S), writes=["CS"], tag="cscp")

        wq_ctr = [0]
        wo_ctr = [0]
        first_pass = [True]

        def experts(sp, tail_cb=None):
            cast_pass = first_pass[0]
            first_pass[0] = False
            for ei in range(NE):
                def emit_ch(ei=ei):
                    def chmm(e, ei=ei):
                        for n in range(NTL):
                            i = e.matmul(pB[:, n * NT:(n + 1) * NT],
                                         SEL[0:8, ei * 128:(ei + 1) * 128],
                                         CS[0:8, n * NT:(n + 1) * NT], start=True, stop=True)
                        return i
                    sched.op("t", chmm, reads=["CS", "SEL"], writes=pb_keys(0, S), tag=f"chmm{ei}")
                    sched.op("s", lambda e: e.copy(CH[:, :], pB[:, 0:S]),
                             reads=pb_keys(0, S), writes=["CH"], tag=f"chcp{ei}")
                for j in range(JC):
                    for half, q in ((0, j), (1, j + JC)):
                        par = wq_ctr[0] % 2
                        wq_ctr[0] += 1
                        if cast_pass:
                            sched.op(f"gx:w1.{par}", lambda e, ei=ei, q=q, par=par: e.dma_start(
                                out=W1B[:, par * KC * 128:(par + 1) * KC * 128], in_=w1_d[ei, q]),
                                writes=[f"W1B.{par}"], tag=f"w1d{ei}.{q}")
                            sched.op(f"gx:wb1.{par}", lambda e, ei=ei, q=q, par=par: e.dma_start(
                                out=w1c_d[ei, q],
                                in_=W1B[:, par * KC * 128:(par + 1) * KC * 128]),
                                reads=[f"W1B.{par}"], writes=[f"w1c.{ei}.{q}"], tag=f"w1wb{ei}.{q}")
                        else:
                            sched.op(f"gx:w1.{par}", lambda e, ei=ei, q=q, par=par: e.dma_start(
                                out=W1B[:, par * KC * 128:(par + 1) * KC * 128], in_=w1c_d[ei, q]),
                                reads=[f"w1c.{ei}.{q}"], writes=[f"W1B.{par}"], tag=f"w1d{ei}.{q}")

                        def m1(e, par=par, half=half):
                            for kc in range(KC):
                                for n in range(NTL):
                                    i = e.matmul(
                                        pA[:, half * S + n * NT:half * S + (n + 1) * NT],
                                        W1B[:, par * KC * 128 + kc * 128:par * KC * 128 + (kc + 1) * 128],
                                        XLN[:, kc * S + n * NT:kc * S + (n + 1) * NT],
                                        start=(kc == 0), stop=(kc == KC - 1))
                            return i
                        sched.op("t", m1, reads=[f"W1B.{par}"] + [f"XLN.{c}" for c in range(KC)],
                                 writes=pa_keys(half * S, (half + 1) * S), tag=f"m1_{ei}.{q}")
                    if j == 0:
                        emit_ch()
                    sched.op("s", lambda e: e.activation(SILU[:, :], pA[:, 0:S], AF.Silu),
                             reads=pa_keys(0, S), writes=["SILU"], tag=f"silu{ei}.{j}")
                    sched.op("v", lambda e: e.tensor_tensor(TMP[:, :], SILU[:, :], CH[:, :], ALU.mult),
                             reads=["SILU", "CH"], writes=["TMP"], tag=f"t1_{ei}.{j}")
                    sched.op("v", lambda e, j=j: e.tensor_tensor(
                        G[:, j * S:(j + 1) * S], TMP[:, :], pA[:, S:2 * S], ALU.mult),
                        reads=["TMP"] + pa_keys(S, 2 * S), writes=[f"G.{j}"], tag=f"g_{ei}.{j}")
                for ob in range(OB):
                    par = wo_ctr[0] % 2
                    wo_ctr[0] += 1
                    if cast_pass:
                        sched.op(f"gx:w2.{par}", lambda e, ei=ei, ob=ob, par=par: e.dma_start(
                            out=W2B[:, par * JC * 128:(par + 1) * JC * 128], in_=w2_d[ei, ob]),
                            writes=[f"W2B.{par}"], tag=f"w2d{ei}.{ob}")
                        sched.op(f"gx:wb2.{par}", lambda e, ei=ei, ob=ob, par=par: e.dma_start(
                            out=w2c_d[ei, ob],
                            in_=W2B[:, par * JC * 128:(par + 1) * JC * 128]),
                            reads=[f"W2B.{par}"], writes=[f"w2c.{ei}.{ob}"], tag=f"w2wb{ei}.{ob}")
                    else:
                        sched.op(f"gx:w2.{par}", lambda e, ei=ei, ob=ob, par=par: e.dma_start(
                            out=W2B[:, par * JC * 128:(par + 1) * JC * 128], in_=w2c_d[ei, ob]),
                            reads=[f"w2c.{ei}.{ob}"], writes=[f"W2B.{par}"], tag=f"w2d{ei}.{ob}")

                    def m2(e, ob=ob, par=par):
                        for jc in range(JC):
                            for n in range(NTL):
                                i = e.matmul(
                                    pB[:, (ob % 2) * S + n * NT:(ob % 2) * S + (n + 1) * NT],
                                    W2B[:, par * JC * 128 + jc * 128:par * JC * 128 + (jc + 1) * 128],
                                    G[:, jc * S + n * NT:jc * S + (n + 1) * NT],
                                    start=(jc == 0), stop=(jc == JC - 1))
                        return i
                    sched.op("t", m2, reads=[f"W2B.{par}"] + [f"G.{j}" for j in range(JC)],
                             writes=pb_keys((ob % 2) * S, (ob % 2 + 1) * S), tag=f"m2_{ei}.{ob}")
                    sched.op("v", lambda e, ob=ob: e.tensor_tensor(
                        X[:, ob * S:(ob + 1) * S], X[:, ob * S:(ob + 1) * S],
                        pB[:, (ob % 2) * S:(ob % 2) * S + S], ALU.add),
                        reads=[f"X.{ob}"] + pb_keys((ob % 2) * S, (ob % 2 + 1) * S),
                        writes=[f"X.{ob}"], tag=f"ev_{ei}.{ob}")
                    if tail_cb is not None and ei == NE - 1:
                        tail_cb(ob)

        def mk_tail(tag):
            def cb(ob):
                stats_sq(tag, ob)
            return cb

        def store_x(sp, dst_r, dstkey):
            for g in range(NGX):
                def st(e, g=g, sp=sp, dst_r=dst_r):
                    return e.dma_start(
                        out=dst_r[:, GXC * g:GXC * (g + 1), sp * S:(sp + 1) * S],
                        in_=X.ap().rearrange("p (c t) -> p c t", t=S)[:, GXC * g:GXC * (g + 1), :])
                sched.op(f"sx:st{g}", st, reads=[f"X.{c}" for c in range(GXC * g, GXC * (g + 1))],
                         writes=[f"{dstkey}.{sp}.{g}"], tag=f"st{dstkey}{sp}.{g}")

        def post_a(sp):
            for c in range(KC):
                stats_xsum(f"post{sp}", c)
                stats_ex2(f"post{sp}", c)
            stats_rows(f"post{sp}", mr=True)
            sched.op("v", lambda e: e.tensor_scalar(R_T4[:, :], R_V[:, :],
                     CST[0:1, C_RT:C_RT + 1], None, ALU.mult),
                     reads=["R_V", "CST"], writes=["R_T4"], tag="t4")
            sched.op("v", lambda e: e.reduce_sum(SMR[0:1, 1:2], R_MR[:, :], axis=AX.X),
                     reads=["R_MR"], writes=["SMR"], tag="smr1")
            sched.op("v", lambda e: e.tensor_scalar(SMR[0:1, 1:2], SMR[0:1, 1:2],
                     CST[0:1, C_RT:C_RT + 1], None, ALU.mult),
                     reads=["SMR", "CST"], writes=["SMR"], tag="smr2")
            sched.op("v", lambda e: e.tensor_tensor(SMR[0:1, 0:1], SMR[0:1, 0:1],
                     SMR[0:1, 1:2], ALU.add),
                     reads=["SMR"], writes=["SMR"], tag="smr3")
            def bct4(e):
                for n in range(NTL):
                    i = e.matmul(pB[:, n * NT:(n + 1) * NT], ONESR[:, :],
                                 R_T4[:, n * NT:(n + 1) * NT], start=True, stop=True)
                return i
            sched.op("t", bct4, reads=["R_T4", "ONESR"], writes=pb_keys(0, S), tag="bct4")
            for c in range(KC):
                sched.op("v", lambda e, c=c, sp=sp: e.scalar_tensor_tensor(
                    TMP[:, :], X[:, c * S:(c + 1) * S], 1.0, pB[:, 0:S],
                    ALU.mult, ALU.mult,
                    accum_out=A1SP[:, sp * KC + c:sp * KC + c + 1]),
                    reads=[f"X.{c}"] + pb_keys(0, S), writes=["TMP", "A1SP"], tag=f"a1m{c}")

        def phase_b():
            sched.op("t", lambda e: e.matmul(pA[:, 0:1], ONESR[:, :], SMR[0:1, 0:1],
                     start=True, stop=True),
                     reads=["SMR", "ONESR"], writes=pa_keys(0, 1), tag="smrbc")
            sched.op("v", lambda e: e.tensor_copy(SMRC[:, :], pA[:, 0:1]),
                     reads=pa_keys(0, 1), writes=["SMRC"], tag="smrcp")
            sched.op("v", lambda e: e.tensor_tensor(A1[:, :], A1SP[:, 0:KC],
                     A1SP[:, KC:2 * KC], ALU.add),
                     reads=["A1SP"], writes=["A1"], tag="a1s0")
            for spp in range(2, NSP):
                sched.op("v", lambda e, spp=spp: e.tensor_tensor(A1[:, :], A1[:, :],
                         A1SP[:, spp * KC:(spp + 1) * KC], ALU.add),
                         reads=["A1", "A1SP"], writes=["A1"], tag=f"a1s{spp}")
            sched.op("v", lambda e: e.tensor_scalar(XBAR[:, :], A1[:, :], SMRC[:, :],
                     None, ALU.subtract),
                     reads=["A1", "SMRC"], writes=["XBAR"], tag="xbar")

            def matvec(wsrc, rhs, rhskey, dst, dstkey, bias, biaskey, tag):
                for oc in range(OB):
                    par = oc % 2
                    sched.op(f"sx:sl{par}", lambda e, oc=oc, par=par, wsrc=wsrc: e.dma_start(
                        out=SLAB[:, par * KC * 128:(par + 1) * KC * 128], in_=wsrc[oc]),
                        writes=[f"SLAB.{par}"], tag=f"{tag}_sl{oc}")
                    pcol = par * min(512, S)

                    def mv(e, par=par, pcol=pcol, rhs=rhs):
                        for kc in range(KC):
                            i = e.matmul(pA[:, pcol:pcol + 1],
                                         SLAB[:, par * KC * 128 + kc * 128:par * KC * 128 + (kc + 1) * 128],
                                         rhs[:, kc:kc + 1],
                                         start=(kc == 0), stop=(kc == KC - 1))
                        return i
                    sched.op("t", mv, reads=[f"SLAB.{par}", rhskey],
                             writes=pa_keys(pcol, pcol + 1), tag=f"{tag}_mv{oc}")
                    sched.op("v", lambda e, oc=oc, pcol=pcol, dst=dst, bias=bias: e.tensor_tensor(
                        dst[:, oc:oc + 1], pA[:, pcol:pcol + 1], bias[:, oc:oc + 1], ALU.add),
                        reads=pa_keys(pcol, pcol + 1) + [biaskey], writes=[dstkey],
                        tag=f"{tag}_ev{oc}")
            matvec(wvt_d, XBAR, "XBAR", VC, "VC", BVT, "BVT", "mvv")
            matvec(wot_d, VC, "VC", AC, "AC", BOT, "BOT", "mvo")

        const_loads()
        for sp in range(NSP):
            load_x(sp, xt_r, None)
            stats(f"pre{sp}")
            ln_and_router(sp)
            experts(sp, tail_cb=mk_tail(f"post{sp}"))
            store_x(sp, xf_r, "xf")
            post_a(sp)
        phase_b()
        for sp in range(NSP):
            load_x(sp, xf_r, "xf")
            add_ac(sp)
            stats(f"c{sp}")
            ln_and_router(sp)
            experts(sp)
            store_x(sp, out_r, "outd")
        fin = sched.op("sx:fin", lambda e: e.nop(),
                       reads=[f"outd.{sp}.{g}" for sp in range(NSP) for g in range(NGX)],
                       tag="final")
        fin.no_inc = True

        counts = sched.finalize()

        sems = {}
        for st in sched.ops:
            sems[st] = E_(nc.semaphore(f"sem_{st.replace(':', '_').replace('.', '_')}"))
        blk = E_(nc.Block())

        def emit(engname):
            def run(eng):
                for o in sched.eng_ops[engname]:
                    for (ws, wc) in o.waits:
                        mult = 16 if sched.is_dma(ws) else 1
                        eng.wait_ge(sems[ws], wc * mult)
                    inst = o.fn(eng)
                    if not o.no_inc:
                        mult = 16 if sched.is_dma(o.stream) else 1
                        inst.then_inc(sems[o.stream], mult)
            return run

        blk.tensor(emit("t"))
        blk.vector(emit("v"))
        blk.scalar(emit("s"))
        blk.gpsimd(emit("gx"))
        blk.sync(emit("sx"))
    return nc


def host_layout(inputs, cfg):
    KC, JC, JB, OB, NE, D, T = cfg.KC, cfg.JC, cfg.JB, cfg.OB, cfg.NE, cfg.D, cfg.T
    F2 = 2 * cfg.F
    x = np.asarray(inputs["L0_attn_x"], dtype=np.float32)
    w1 = np.asarray(inputs["L0_ffn_w1"], dtype=np.float32)
    w2 = np.asarray(inputs["L0_ffn_w2"], dtype=np.float32)
    wg = np.asarray(inputs["L0_ffn_wg"], dtype=np.float32)
    Wv = np.asarray(inputs["L0_attn_Wv"], dtype=np.float32)
    Wo = np.asarray(inputs["L0_attn_Wo"], dtype=np.float32)
    bv = np.asarray(inputs["L0_attn_bv"], dtype=np.float32)
    bo = np.asarray(inputs["L0_attn_bo"], dtype=np.float32)

    w1r = np.ascontiguousarray(
        w1.reshape(NE, KC, 128, JB, 128).transpose(0, 3, 2, 1, 4)).reshape(NE, JB, 128, KC * 128)
    w2r = np.ascontiguousarray(
        w2.reshape(NE, JC, 128, OB, 128).transpose(0, 3, 2, 1, 4)).reshape(NE, OB, 128, JC * 128)
    wgt = np.ascontiguousarray(wg.T.reshape(KC, 128, NE).transpose(1, 0, 2)).reshape(128, KC * NE)
    wvt = np.ascontiguousarray(
        Wv.T.reshape(KC, 128, OB, 128).transpose(2, 1, 0, 3)).reshape(OB, 128, KC * 128)
    wot = np.ascontiguousarray(
        Wo.T.reshape(KC, 128, OB, 128).transpose(2, 1, 0, 3)).reshape(OB, 128, KC * 128)
    bvt = np.ascontiguousarray(bv.reshape(KC, 128).T)
    bot = np.ascontiguousarray(bo.reshape(KC, 128).T)

    sel = np.zeros((8, NE * 128), np.float32)
    for e in range(NE):
        sel[e, e * 128:(e + 1) * 128] = 0.5
    idn = np.eye(128, dtype=np.float32)
    cst = np.zeros((128, 8), np.float32)
    cst[:, C_EPS] = 1e-5
    cst[:, C_NHALF] = -0.5
    cst[:, C_RD] = 1.0 / D
    cst[:, C_RT] = 1.0 / T
    cst[:, C_ONE] = 1.0
    onesr = np.ones((1, 128), np.float32)
    onesk = np.ones((128, 1), np.float32)

    common = {"w1r": w1r, "w2r": w2r, "wgt": wgt, "wvt": wvt, "wot": wot,
              "bvt": bvt, "bot": bot, "sel": sel, "idn": idn, "cst": cst,
              "onesr": onesr, "onesk": onesk}
    in_maps = []
    for c in range(x.shape[0]):
        m = dict(common)
        m["xt"] = np.ascontiguousarray(x[c].T)
        in_maps.append(m)
    return in_maps


def kernel(**inputs):
    cfg = CFG_REAL
    in_maps = host_layout(inputs, cfg)
    nc = build_program(cfg)
    res = run_bass_kernel_spmd(nc, in_maps, core_ids=list(range(8)))
    outs = [np.asarray(r["out"]) for r in res.results]
    full = np.stack(outs, axis=0).transpose(0, 2, 1)
    return np.ascontiguousarray(full.astype(np.float32))


if __name__ == "__main__":
    cfg = CFG_REAL
    import time
    t0 = time.time()
    nc = build_program(cfg)
    try:
        n = nc.n_instructions()
    except Exception:
        n = "?"
    print("build s:", time.time() - t0, "n_inst:", n)


# version 08
# speedup vs baseline: 1.0379x; 1.0379x over previous
import sys
if "/opt/trn_rl_repo" not in sys.path:
    sys.path.insert(0, "/opt/trn_rl_repo")
import numpy as np
import concourse.bass as bass
import concourse.mybir as mybir
from concourse.bass_utils import run_bass_kernel_spmd

F32 = mybir.dt.float32
BF16 = mybir.dt.bfloat16
AF = mybir.ActivationFunctionType
ALU = mybir.AluOpType
AX = mybir.AxisListType


class Cfg:
    def __init__(self, KC, JC, S, NSP):
        self.KC = KC
        self.JC = JC
        self.JB = 2 * JC
        self.OB = KC
        self.NE = 8
        self.S = S
        self.NSP = NSP
        self.T = S * NSP
        self.D = 128 * KC
        self.F = 128 * JC
        self.NT = min(512, S)
        self.GX = min(4, KC)
        self.NGX = KC // self.GX
        self.NTL = S // self.NT
        self.TT = S // 128


CFG_REAL = Cfg(KC=16, JC=11, S=1024, NSP=4)

C_EPS = 0
C_NHALF = 1
C_RD = 2
C_RT = 3
C_ONE = 4


class Op:
    __slots__ = ("stream", "engine", "fn", "idx", "waits", "tag", "no_inc")

    def __init__(self, stream, engine, fn, idx, tag):
        self.stream = stream
        self.engine = engine
        self.fn = fn
        self.idx = idx
        self.waits = []
        self.tag = tag
        self.no_inc = False


class Sched:
    ENGINES = ("t", "v", "s", "gx", "sx")
    DMA_ENG = frozenset(("gx", "sx"))

    def __init__(self):
        self.ops = {}
        self.eng_ops = {e: [] for e in self.ENGINES}
        self.order = []
        self.last_writer = {}
        self.readers = {}
        self.vc = {}
        self.engine_vc = {e: {} for e in self.ENGINES}
        self.snap = {}

    def is_dma(self, stream):
        return stream.split(":")[0] in self.DMA_ENG

    def op(self, stream, fn, reads=(), writes=(), tag=""):
        engine = stream.split(":")[0]
        lst = self.ops.setdefault(stream, [])
        o = Op(stream, engine, fn, len(lst), tag)
        lst.append(o)
        self.eng_ops[engine].append(o)
        self.order.append(o)
        deps = set()
        for b in reads:
            w = self.last_writer.get(b)
            if w is not None:
                deps.add(w)
            self.readers.setdefault(b, []).append(o)
        for b in writes:
            w = self.last_writer.get(b)
            if w is not None:
                deps.add(w)
            for r in self.readers.get(b, ()):
                deps.add(r)
            self.readers[b] = []
            self.last_writer[b] = o
        deps.discard(o)
        V = self.engine_vc[engine]
        for d in sorted(deps, key=lambda d: (d.stream, d.idx)):
            if d.engine == "t" and engine == "t":
                continue
            if d.engine == engine and d.stream == stream and not self.is_dma(stream):
                pass
            need = d.idx + 1
            if V.get(d.stream, 0) >= need:
                continue
            o.waits.append((d.stream, need))
            for k, v in self.snap[d].items():
                if V.get(k, 0) < v:
                    V[k] = v
            if V.get(d.stream, 0) < need:
                V[d.stream] = need
        s = dict(V)
        s[stream] = o.idx + 1
        self.snap[o] = s
        return o

    def finalize(self):
        for o in self.order:
            for (ws, wc) in o.waits:
                assert wc <= len(self.ops[ws]), (o.tag, ws, wc)
        return {st: len(v) for st, v in self.ops.items()}


def build_program(cfg):
    KC, JC, JB, OB, NE = cfg.KC, cfg.JC, cfg.JB, cfg.OB, cfg.NE
    S, NSP, T, D = cfg.S, cfg.NSP, cfg.T, cfg.D
    NT, NTL, TT = cfg.NT, cfg.NTL, cfg.TT
    EXOFF = max(S, 512)

    nc = bass.Bass("TRN2", target_bir_lowering=False)
    xt_d = nc.dram_tensor("xt", [D, T], F32, kind="ExternalInput").ap()
    w1_d = nc.dram_tensor("w1r", [NE, JB, 128, KC * 128], F32, kind="ExternalInput").ap()
    w2_d = nc.dram_tensor("w2r", [NE, OB, 128, JC * 128], F32, kind="ExternalInput").ap()
    wgt_d = nc.dram_tensor("wgt", [128, KC * 8], F32, kind="ExternalInput").ap()
    wvt_d = nc.dram_tensor("wvt", [OB, 128, KC * 128], F32, kind="ExternalInput").ap()
    wot_d = nc.dram_tensor("wot", [OB, 128, KC * 128], F32, kind="ExternalInput").ap()
    bvt_d = nc.dram_tensor("bvt", [128, KC], F32, kind="ExternalInput").ap()
    bot_d = nc.dram_tensor("bot", [128, KC], F32, kind="ExternalInput").ap()
    sel_d = nc.dram_tensor("sel", [8, NE * 128], F32, kind="ExternalInput").ap()
    idn_d = nc.dram_tensor("idn", [128, 128], F32, kind="ExternalInput").ap()
    cst_d = nc.dram_tensor("cst", [128, 8], F32, kind="ExternalInput").ap()
    onesr_d = nc.dram_tensor("onesr", [1, 128], F32, kind="ExternalInput").ap()
    onesk_d = nc.dram_tensor("onesk", [128, 1], F32, kind="ExternalInput").ap()
    out_d = nc.dram_tensor("out", [D, T], F32, kind="ExternalOutput").ap()
    xf_d = nc.dram_tensor("xf", [D, T], F32, kind="Internal").ap()
    w1c_d = nc.dram_tensor("w1c", [NE, JB, 128, KC * 128], BF16, kind="Internal").ap()
    w2c_d = nc.dram_tensor("w2c", [NE, OB, 128, JC * 128], BF16, kind="Internal").ap()

    xt_r = xt_d.rearrange("(c p) t -> p c t", p=128)
    xf_r = xf_d.rearrange("(c p) t -> p c t", p=128)
    out_r = out_d.rearrange("(c p) t -> p c t", p=128)

    from contextlib import ExitStack
    with ExitStack() as ctx:
        E_ = ctx.enter_context
        X = E_(nc.sbuf_tensor("X", [128, KC * S], F32))
        XLN = E_(nc.sbuf_tensor("XLN", [128, KC * S], BF16))
        G = E_(nc.sbuf_tensor("G", [128, JC * S], BF16))
        SQ = E_(nc.sbuf_tensor("SQ", [128, 4 * S], BF16))
        SILU = E_(nc.sbuf_tensor("SILU", [128, S], F32))
        TMP = E_(nc.sbuf_tensor("TMP", [128, S], F32))
        CH = E_(nc.sbuf_tensor("CH", [128, S], F32))
        W1B = E_(nc.sbuf_tensor("W1B", [128, 2 * KC * 128], BF16))
        W2B = E_(nc.sbuf_tensor("W2B", [128, 2 * JC * 128], BF16))
        LG = E_(nc.sbuf_tensor("LG", [8, S], F32))
        CS = E_(nc.sbuf_tensor("CS", [8, S], F32))
        R_MU = E_(nc.sbuf_tensor("R_MU", [1, S], F32))
        R_V = E_(nc.sbuf_tensor("R_V", [1, S], F32))
        R_MR = E_(nc.sbuf_tensor("R_MR", [1, S], F32))
        R_T4 = E_(nc.sbuf_tensor("R_T4", [1, S], F32))
        RT = E_(nc.sbuf_tensor("RT", [128, 8 * TT], F32))
        DD = E_(nc.sbuf_tensor("DD", [128, 8 * TT], F32))
        SELS = E_(nc.sbuf_tensor("SELS", [128, 8 * TT], F32))
        EE = E_(nc.sbuf_tensor("EE", [128, 8 * TT], F32))
        CTT = E_(nc.sbuf_tensor("CTT", [128, 8 * TT], F32))
        MRB = E_(nc.sbuf_tensor("MRB", [128, 8], F32))
        COLS = E_(nc.sbuf_tensor("COLS", [128, 4], F32))
        DL2S = E_(nc.sbuf_tensor("DL2S", [128, TT], F32))
        E2 = E_(nc.sbuf_tensor("E2", [128, TT], F32))
        DEN = E_(nc.sbuf_tensor("DEN", [128, TT], F32))
        RDEN = E_(nc.sbuf_tensor("RDEN", [128, TT], F32))
        A1 = E_(nc.sbuf_tensor("A1", [128, KC], F32))
        A1SP = E_(nc.sbuf_tensor("A1SP", [128, NSP * KC], F32))
        SMR = E_(nc.sbuf_tensor("SMR", [1, 2], F32))
        SMRC = E_(nc.sbuf_tensor("SMRC", [128, 1], F32))
        XBAR = E_(nc.sbuf_tensor("XBAR", [128, KC], F32))
        VC = E_(nc.sbuf_tensor("VC", [128, KC], F32))
        AC = E_(nc.sbuf_tensor("AC", [128, KC], F32))
        BVT = E_(nc.sbuf_tensor("BVT", [128, KC], F32))
        BOT = E_(nc.sbuf_tensor("BOT", [128, KC], F32))
        SLAB = E_(nc.sbuf_tensor("SLAB", [128, 2 * KC * 128], F32))
        CST = E_(nc.sbuf_tensor("CST", [128, 8], F32))
        IDN = E_(nc.sbuf_tensor("IDN", [128, 128], F32))
        SEL = E_(nc.sbuf_tensor("SEL", [8, NE * 128], F32))
        WGTB = E_(nc.sbuf_tensor("WGTB", [128, KC * 8], BF16))
        ONESR = E_(nc.sbuf_tensor("ONESR", [1, 128], F32))
        ONESK = E_(nc.sbuf_tensor("ONESK", [128, 1], F32))
        ONESKB = E_(nc.sbuf_tensor("ONESKB", [128, 1], BF16))
        pA = E_(nc.psum_tensor("pA", [128, max(2 * S, EXOFF + max(S, 512))], F32))
        pB = E_(nc.psum_tensor("pB", [128, 2 * S], F32))

        sched = Sched()

        def pa_keys(lo, hi):
            return [f"pA.{b}" for b in range(lo // 512, (hi - 1) // 512 + 1)]

        def pb_keys(lo, hi):
            return [f"pB.{b}" for b in range(lo // 512, (hi - 1) // 512 + 1)]

        def const_loads():
            pairs = [(CST, cst_d, "CST"), (IDN, idn_d, "IDN"), (SEL, sel_d, "SEL"),
                     (ONESR, onesr_d, "ONESR"),
                     (ONESK, onesk_d, "ONESK"), (BVT, bvt_d, "BVT"), (BOT, bot_d, "BOT")]
            for tb, src, key in pairs:
                sched.op(f"sx:c.{key}", lambda e, tb=tb, src=src: e.dma_start(out=tb[:, :], in_=src),
                         writes=[key], tag=f"ld_{key}")
            sched.op("gx:c.WGTB", lambda e: e.dma_start(out=WGTB[:, :], in_=wgt_d),
                     writes=["WGTB"], tag="ld_WGTB")
            sched.op("gx:c.ONESKB", lambda e: e.dma_start(out=ONESKB[:, :], in_=onesk_d),
                     writes=["ONESKB"], tag="ld_ONESKB")
            sched.op("v", lambda e: e.memset(SMR[:, :], 0.0), writes=["SMR"], tag="z_SMR")

        GXC, NGX = cfg.GX, cfg.NGX

        def load_x(sp, src_r, srckey):
            for g in range(NGX):
                def ld(e, g=g, sp=sp, src_r=src_r):
                    return e.dma_start(
                        out=X.ap().rearrange("p (c t) -> p c t", t=S)[:, GXC * g:GXC * (g + 1), :],
                        in_=src_r[:, GXC * g:GXC * (g + 1), sp * S:(sp + 1) * S])
                rd = [] if srckey is None else [f"{srckey}.{sp}.{g}"]
                sched.op(f"sx:x{g}", ld, reads=rd,
                         writes=[f"X.{c}" for c in range(GXC * g, GXC * (g + 1))], tag=f"ldx{sp}.{g}")

        def add_ac(sp):
            for c in range(KC):
                sched.op("v", lambda e, c=c: e.tensor_scalar(
                    X[:, c * S:(c + 1) * S], X[:, c * S:(c + 1) * S],
                    AC[:, c:c + 1], None, ALU.add),
                    reads=[f"X.{c}", "AC"], writes=[f"X.{c}"], tag=f"addac{c}")

        def stats_sq(tag, c, eng="v"):
            p = c % 4
            if eng == "s":
                sched.op("s", lambda e, c=c, p=p: e.activation(
                    SQ[:, p * S:(p + 1) * S], X[:, c * S:(c + 1) * S], AF.Square),
                    reads=[f"X.{c}"], writes=[f"SQ.{p}"], tag=f"{tag}_sq{c}")
            else:
                sched.op("v", lambda e, c=c, p=p: e.tensor_tensor(
                    SQ[:, p * S:(p + 1) * S], X[:, c * S:(c + 1) * S],
                    X[:, c * S:(c + 1) * S], ALU.mult),
                    reads=[f"X.{c}"], writes=[f"SQ.{p}"], tag=f"{tag}_sq{c}")

        def stats_xsum(tag, c):
            def smm(e, c=c):
                for n in range(NTL):
                    i = e.matmul(pA[0:1, n * NT:(n + 1) * NT], ONESK[:, :],
                                 X[:, c * S + n * NT:c * S + (n + 1) * NT],
                                 start=(c == 0), stop=(c == KC - 1))
                return i
            sched.op("t", smm, reads=[f"X.{c}", "ONESK"],
                     writes=pa_keys(0, S), tag=f"{tag}_xmm{c}")

        def stats_ex2(tag, c):
            p = c % 4
            def smm(e, c=c, p=p):
                for n in range(NTL):
                    i = e.matmul(pA[0:1, EXOFF + n * NT:EXOFF + (n + 1) * NT], ONESKB[:, :],
                                 SQ[:, p * S + n * NT:p * S + (n + 1) * NT],
                                 start=(c == 0), stop=(c == KC - 1))
                return i
            sched.op("t", smm, reads=[f"SQ.{p}", "ONESKB"],
                     writes=pa_keys(EXOFF, EXOFF + S), tag=f"{tag}_emm{c}")

        def stats_rows(tag, mr):
            sched.op("v", lambda e: e.tensor_scalar(R_MU[:, :], pA[0:1, 0:S],
                     CST[0:1, C_RD:C_RD + 1], None, ALU.mult),
                     reads=pa_keys(0, S) + ["CST"], writes=["R_MU"], tag=f"{tag}_rmu")
            sched.op("v", lambda e: e.tensor_tensor(R_T4[:, :], R_MU[:, :], R_MU[:, :], ALU.mult),
                     reads=["R_MU"], writes=["R_T4"], tag=f"{tag}_rmu2")
            sched.op("v", lambda e: e.scalar_tensor_tensor(
                R_V[:, :], pA[0:1, EXOFF:EXOFF + S], CST[0:1, C_RD:C_RD + 1], R_T4[:, :],
                ALU.mult, ALU.subtract),
                reads=pa_keys(EXOFF, EXOFF + S) + ["CST", "R_T4"], writes=["R_V"], tag=f"{tag}_rvar")
            sched.op("s", lambda e: e.activation(R_V[:, :], R_V[:, :], AF.Ln,
                     bias=CST[0:1, C_EPS:C_EPS + 1]),
                     reads=["R_V", "CST"], writes=["R_V"], tag=f"{tag}_rln")
            sched.op("s", lambda e: e.activation(R_V[:, :], R_V[:, :], AF.Exp,
                     scale=CST[0:1, C_NHALF:C_NHALF + 1]),
                     reads=["R_V", "CST"], writes=["R_V"], tag=f"{tag}_rexp")
            if mr:
                sched.op("v", lambda e: e.tensor_tensor(R_MR[:, :], R_MU[:, :], R_V[:, :], ALU.mult),
                         reads=["R_MU", "R_V"], writes=["R_MR"], tag=f"{tag}_rmr")

        def stats(tag, mr=False):
            for c in range(KC):
                stats_sq(tag, c, eng=("s" if c % 2 == 0 else "v"))
                stats_xsum(tag, c)
                stats_ex2(tag, c)
            stats_rows(tag, mr)

        def ln_and_router(sp):
            def bc_mu(e):
                for n in range(NTL):
                    i = e.matmul(pB[:, n * NT:(n + 1) * NT], ONESR[:, :],
                                 R_MU[:, n * NT:(n + 1) * NT], start=True, stop=True)
                return i
            sched.op("t", bc_mu, reads=["R_MU", "ONESR"], writes=pb_keys(0, S), tag="lnbcmu")

            def bc_rs(e):
                for n in range(NTL):
                    i = e.matmul(pB[:, S + n * NT:S + (n + 1) * NT], ONESR[:, :],
                                 R_V[:, n * NT:(n + 1) * NT], start=True, stop=True)
                return i
            sched.op("t", bc_rs, reads=["R_V", "ONESR"], writes=pb_keys(S, 2 * S), tag="lnbcrs")
            for c in range(KC):
                sched.op("v", lambda e, c=c: e.tensor_tensor(
                    TMP[:, :], X[:, c * S:(c + 1) * S], pB[:, 0:S], ALU.subtract),
                    reads=[f"X.{c}"] + pb_keys(0, S), writes=["TMP"], tag=f"lna{c}")
                sched.op("v", lambda e, c=c: e.tensor_tensor(
                    XLN[:, c * S:(c + 1) * S], TMP[:, :], pB[:, S:2 * S], ALU.mult),
                    reads=["TMP"] + pb_keys(S, 2 * S), writes=[f"XLN.{c}"], tag=f"lnb{c}")
            for c in range(KC):
                def rmm(e, c=c):
                    for n in range(NTL):
                        i = e.matmul(pA[0:8, n * NT:(n + 1) * NT], WGTB[:, c * 8:(c + 1) * 8],
                                     XLN[:, c * S + n * NT:c * S + (n + 1) * NT],
                                     start=(c == 0), stop=(c == KC - 1))
                    return i
                sched.op("t", rmm, reads=[f"XLN.{c}", "WGTB"], writes=pa_keys(0, S), tag=f"rmm{c}")
            sched.op("s", lambda e: e.copy(LG[:, :], pA[0:8, 0:S]),
                     reads=pa_keys(0, S), writes=["LG"], tag="lgcp")
            def tp(e):
                for t_ in range(TT):
                    i = e.transpose(pA[:, S + 8 * t_:S + 8 * t_ + 8],
                                    LG[0:8, t_ * 128:(t_ + 1) * 128], IDN[0:8, 0:8])
                return i
            sched.op("t", tp, reads=["LG", "IDN"], writes=pa_keys(S, S + 8 * TT), tag="lgtp")
            sched.op("v", lambda e: e.tensor_copy(RT[:, :], pA[:, S:S + 8 * TT]),
                     reads=pa_keys(S, S + 8 * TT), writes=["RT"], tag="rtcp")
            for t_ in range(TT):
                rt = RT[:, 8 * t_:8 * t_ + 8]
                sched.op("v", lambda e, rt=rt: e.max(out=MRB[:, :], in_=rt),
                         reads=["RT"], writes=["MRB"], tag=f"m8_{t_}")
                sched.op("v", lambda e, rt=rt, t_=t_: e.tensor_scalar(
                    DD[:, 8 * t_:8 * t_ + 8], rt, MRB[:, 0:1], None, ALU.subtract),
                    reads=["RT", "MRB"], writes=["DD"], tag=f"dd_{t_}")
                sched.op("v", lambda e, t_=t_: e.tensor_tensor(
                    DL2S[:, t_:t_ + 1], MRB[:, 1:2], MRB[:, 0:1], ALU.subtract),
                    reads=["MRB"], writes=["DL2S"], tag=f"dl2_{t_}")
                sched.op("v", lambda e, t_=t_: e.tensor_scalar(
                    SELS[:, 8 * t_:8 * t_ + 8], DD[:, 8 * t_:8 * t_ + 8],
                    DL2S[:, t_:t_ + 1], None, ALU.is_ge),
                    reads=["DD", "DL2S"], writes=["SELS"], tag=f"sel_{t_}")
            sched.op("s", lambda e: e.activation(EE[:, :], DD[:, :], AF.Exp),
                     reads=["DD"], writes=["EE"], tag="ee")
            sched.op("s", lambda e: e.activation(E2[:, :], DL2S[:, :], AF.Exp),
                     reads=["DL2S"], writes=["E2"], tag="e2")
            sched.op("v", lambda e: e.tensor_scalar(DEN[:, :], E2[:, :],
                     CST[:, C_ONE:C_ONE + 1], None, ALU.add),
                     reads=["E2", "CST"], writes=["DEN"], tag="den")
            sched.op("v", lambda e: e.reciprocal(RDEN[:, :], DEN[:, :]),
                     reads=["DEN"], writes=["RDEN"], tag="rden")
            for t_ in range(TT):
                sched.op("v", lambda e, t_=t_: e.tensor_scalar(
                    CTT[:, 8 * t_:8 * t_ + 8], EE[:, 8 * t_:8 * t_ + 8],
                    RDEN[:, t_:t_ + 1], None, ALU.mult),
                    reads=["EE", "RDEN"], writes=["CTT"], tag=f"ctw_{t_}")
                sched.op("v", lambda e, t_=t_: e.tensor_tensor(
                    CTT[:, 8 * t_:8 * t_ + 8], CTT[:, 8 * t_:8 * t_ + 8],
                    SELS[:, 8 * t_:8 * t_ + 8], ALU.mult),
                    reads=["CTT", "SELS"], writes=["CTT"], tag=f"cts_{t_}")
            def tpb(e):
                for t_ in range(TT):
                    i = e.transpose(pA[0:8, t_ * 128:(t_ + 1) * 128],
                                    CTT[:, 8 * t_:8 * t_ + 8], IDN[:, :])
                return i
            sched.op("t", tpb, reads=["CTT", "IDN"], writes=pa_keys(0, S), tag="cttp")
            sched.op("s", lambda e: e.copy(CS[:, :], pA[0:8, 0:S]),
                     reads=pa_keys(0, S), writes=["CS"], tag="cscp")

        wq_ctr = [0]
        wo_ctr = [0]
        first_pass = [True]

        def experts(sp, tail_cb=None):
            cast_pass = first_pass[0]
            first_pass[0] = False
            for ei in range(NE):
                def emit_ch(ei=ei):
                    def chmm(e, ei=ei):
                        for n in range(NTL):
                            i = e.matmul(pB[:, n * NT:(n + 1) * NT],
                                         SEL[0:8, ei * 128:(ei + 1) * 128],
                                         CS[0:8, n * NT:(n + 1) * NT], start=True, stop=True)
                        return i
                    sched.op("t", chmm, reads=["CS", "SEL"], writes=pb_keys(0, S), tag=f"chmm{ei}")
                    sched.op("s", lambda e: e.copy(CH[:, :], pB[:, 0:S]),
                             reads=pb_keys(0, S), writes=["CH"], tag=f"chcp{ei}")
                for j in range(JC):
                    for half, q in ((0, j), (1, j + JC)):
                        par = wq_ctr[0] % 2
                        wq_ctr[0] += 1
                        if cast_pass:
                            sched.op(f"gx:w1.{par}", lambda e, ei=ei, q=q, par=par: e.dma_start(
                                out=W1B[:, par * KC * 128:(par + 1) * KC * 128], in_=w1_d[ei, q]),
                                writes=[f"W1B.{par}"], tag=f"w1d{ei}.{q}")
                            sched.op(f"gx:wb1.{par}", lambda e, ei=ei, q=q, par=par: e.dma_start(
                                out=w1c_d[ei, q],
                                in_=W1B[:, par * KC * 128:(par + 1) * KC * 128]),
                                reads=[f"W1B.{par}"], writes=[f"w1c.{ei}.{q}"], tag=f"w1wb{ei}.{q}")
                        else:
                            sched.op(f"gx:w1.{par}", lambda e, ei=ei, q=q, par=par: e.dma_start(
                                out=W1B[:, par * KC * 128:(par + 1) * KC * 128], in_=w1c_d[ei, q]),
                                reads=[f"w1c.{ei}.{q}"], writes=[f"W1B.{par}"], tag=f"w1d{ei}.{q}")

                        def m1(e, par=par, half=half):
                            for kc in range(KC):
                                for n in range(NTL):
                                    i = e.matmul(
                                        pA[:, half * S + n * NT:half * S + (n + 1) * NT],
                                        W1B[:, par * KC * 128 + kc * 128:par * KC * 128 + (kc + 1) * 128],
                                        XLN[:, kc * S + n * NT:kc * S + (n + 1) * NT],
                                        start=(kc == 0), stop=(kc == KC - 1))
                            return i
                        sched.op("t", m1, reads=[f"W1B.{par}"] + [f"XLN.{c}" for c in range(KC)],
                                 writes=pa_keys(half * S, (half + 1) * S), tag=f"m1_{ei}.{q}")
                    if j == 0:
                        emit_ch()
                    sched.op("s", lambda e: e.activation(SILU[:, :], pA[:, 0:S], AF.Silu),
                             reads=pa_keys(0, S), writes=["SILU"], tag=f"silu{ei}.{j}")
                    sched.op("v", lambda e: e.tensor_tensor(TMP[:, :], SILU[:, :], CH[:, :], ALU.mult),
                             reads=["SILU", "CH"], writes=["TMP"], tag=f"t1_{ei}.{j}")
                    sched.op("v", lambda e, j=j: e.tensor_tensor(
                        G[:, j * S:(j + 1) * S], TMP[:, :], pA[:, S:2 * S], ALU.mult),
                        reads=["TMP"] + pa_keys(S, 2 * S), writes=[f"G.{j}"], tag=f"g_{ei}.{j}")
                for ob in range(OB):
                    par = wo_ctr[0] % 2
                    wo_ctr[0] += 1
                    if cast_pass:
                        sched.op(f"gx:w2.{par}", lambda e, ei=ei, ob=ob, par=par: e.dma_start(
                            out=W2B[:, par * JC * 128:(par + 1) * JC * 128], in_=w2_d[ei, ob]),
                            writes=[f"W2B.{par}"], tag=f"w2d{ei}.{ob}")
                        sched.op(f"gx:wb2.{par}", lambda e, ei=ei, ob=ob, par=par: e.dma_start(
                            out=w2c_d[ei, ob],
                            in_=W2B[:, par * JC * 128:(par + 1) * JC * 128]),
                            reads=[f"W2B.{par}"], writes=[f"w2c.{ei}.{ob}"], tag=f"w2wb{ei}.{ob}")
                    else:
                        sched.op(f"gx:w2.{par}", lambda e, ei=ei, ob=ob, par=par: e.dma_start(
                            out=W2B[:, par * JC * 128:(par + 1) * JC * 128], in_=w2c_d[ei, ob]),
                            reads=[f"w2c.{ei}.{ob}"], writes=[f"W2B.{par}"], tag=f"w2d{ei}.{ob}")

                    def m2(e, ob=ob, par=par):
                        for jc in range(JC):
                            for n in range(NTL):
                                i = e.matmul(
                                    pB[:, (ob % 2) * S + n * NT:(ob % 2) * S + (n + 1) * NT],
                                    W2B[:, par * JC * 128 + jc * 128:par * JC * 128 + (jc + 1) * 128],
                                    G[:, jc * S + n * NT:jc * S + (n + 1) * NT],
                                    start=(jc == 0), stop=(jc == JC - 1))
                        return i
                    sched.op("t", m2, reads=[f"W2B.{par}"] + [f"G.{j}" for j in range(JC)],
                             writes=pb_keys((ob % 2) * S, (ob % 2 + 1) * S), tag=f"m2_{ei}.{ob}")
                    sched.op("v", lambda e, ob=ob: e.tensor_tensor(
                        X[:, ob * S:(ob + 1) * S], X[:, ob * S:(ob + 1) * S],
                        pB[:, (ob % 2) * S:(ob % 2) * S + S], ALU.add),
                        reads=[f"X.{ob}"] + pb_keys((ob % 2) * S, (ob % 2 + 1) * S),
                        writes=[f"X.{ob}"], tag=f"ev_{ei}.{ob}")
                    if tail_cb is not None and ei == NE - 1:
                        tail_cb(ob)

        def mk_tail(tag):
            def cb(ob):
                stats_sq(tag, ob)
            return cb

        def store_x(sp, dst_r, dstkey):
            for g in range(NGX):
                def st(e, g=g, sp=sp, dst_r=dst_r):
                    return e.dma_start(
                        out=dst_r[:, GXC * g:GXC * (g + 1), sp * S:(sp + 1) * S],
                        in_=X.ap().rearrange("p (c t) -> p c t", t=S)[:, GXC * g:GXC * (g + 1), :])
                sched.op(f"sx:st{g}", st, reads=[f"X.{c}" for c in range(GXC * g, GXC * (g + 1))],
                         writes=[f"{dstkey}.{sp}.{g}"], tag=f"st{dstkey}{sp}.{g}")

        def post_a(sp):
            for c in range(KC):
                stats_xsum(f"post{sp}", c)
                stats_ex2(f"post{sp}", c)
            stats_rows(f"post{sp}", mr=True)
            sched.op("v", lambda e: e.tensor_scalar(R_T4[:, :], R_V[:, :],
                     CST[0:1, C_RT:C_RT + 1], None, ALU.mult),
                     reads=["R_V", "CST"], writes=["R_T4"], tag="t4")
            sched.op("v", lambda e: e.reduce_sum(SMR[0:1, 1:2], R_MR[:, :], axis=AX.X),
                     reads=["R_MR"], writes=["SMR"], tag="smr1")
            sched.op("v", lambda e: e.tensor_scalar(SMR[0:1, 1:2], SMR[0:1, 1:2],
                     CST[0:1, C_RT:C_RT + 1], None, ALU.mult),
                     reads=["SMR", "CST"], writes=["SMR"], tag="smr2")
            sched.op("v", lambda e: e.tensor_tensor(SMR[0:1, 0:1], SMR[0:1, 0:1],
                     SMR[0:1, 1:2], ALU.add),
                     reads=["SMR"], writes=["SMR"], tag="smr3")
            def bct4(e):
                for n in range(NTL):
                    i = e.matmul(pB[:, n * NT:(n + 1) * NT], ONESR[:, :],
                                 R_T4[:, n * NT:(n + 1) * NT], start=True, stop=True)
                return i
            sched.op("t", bct4, reads=["R_T4", "ONESR"], writes=pb_keys(0, S), tag="bct4")
            for c in range(KC):
                sched.op("v", lambda e, c=c, sp=sp: e.scalar_tensor_tensor(
                    TMP[:, :], X[:, c * S:(c + 1) * S], 1.0, pB[:, 0:S],
                    ALU.mult, ALU.mult,
                    accum_out=A1SP[:, sp * KC + c:sp * KC + c + 1]),
                    reads=[f"X.{c}"] + pb_keys(0, S), writes=["TMP", "A1SP"], tag=f"a1m{c}")

        def phase_b():
            sched.op("t", lambda e: e.matmul(pA[:, 0:1], ONESR[:, :], SMR[0:1, 0:1],
                     start=True, stop=True),
                     reads=["SMR", "ONESR"], writes=pa_keys(0, 1), tag="smrbc")
            sched.op("v", lambda e: e.tensor_copy(SMRC[:, :], pA[:, 0:1]),
                     reads=pa_keys(0, 1), writes=["SMRC"], tag="smrcp")
            sched.op("v", lambda e: e.tensor_tensor(A1[:, :], A1SP[:, 0:KC],
                     A1SP[:, KC:2 * KC], ALU.add),
                     reads=["A1SP"], writes=["A1"], tag="a1s0")
            for spp in range(2, NSP):
                sched.op("v", lambda e, spp=spp: e.tensor_tensor(A1[:, :], A1[:, :],
                         A1SP[:, spp * KC:(spp + 1) * KC], ALU.add),
                         reads=["A1", "A1SP"], writes=["A1"], tag=f"a1s{spp}")
            sched.op("v", lambda e: e.tensor_scalar(XBAR[:, :], A1[:, :], SMRC[:, :],
                     None, ALU.subtract),
                     reads=["A1", "SMRC"], writes=["XBAR"], tag="xbar")

            def matvec(wsrc, rhs, rhskey, dst, dstkey, bias, biaskey, tag):
                for oc in range(OB):
                    par = oc % 2
                    sched.op(f"sx:sl{par}", lambda e, oc=oc, par=par, wsrc=wsrc: e.dma_start(
                        out=SLAB[:, par * KC * 128:(par + 1) * KC * 128], in_=wsrc[oc]),
                        writes=[f"SLAB.{par}"], tag=f"{tag}_sl{oc}")
                    pcol = par * min(512, S)

                    def mv(e, par=par, pcol=pcol, rhs=rhs):
                        for kc in range(KC):
                            i = e.matmul(pA[:, pcol:pcol + 1],
                                         SLAB[:, par * KC * 128 + kc * 128:par * KC * 128 + (kc + 1) * 128],
                                         rhs[:, kc:kc + 1],
                                         start=(kc == 0), stop=(kc == KC - 1))
                        return i
                    sched.op("t", mv, reads=[f"SLAB.{par}", rhskey],
                             writes=pa_keys(pcol, pcol + 1), tag=f"{tag}_mv{oc}")
                    sched.op("v", lambda e, oc=oc, pcol=pcol, dst=dst, bias=bias: e.tensor_tensor(
                        dst[:, oc:oc + 1], pA[:, pcol:pcol + 1], bias[:, oc:oc + 1], ALU.add),
                        reads=pa_keys(pcol, pcol + 1) + [biaskey], writes=[dstkey],
                        tag=f"{tag}_ev{oc}")
            matvec(wvt_d, XBAR, "XBAR", VC, "VC", BVT, "BVT", "mvv")
            matvec(wot_d, VC, "VC", AC, "AC", BOT, "BOT", "mvo")

        const_loads()
        for sp in range(NSP):
            load_x(sp, xt_r, None)
            stats(f"pre{sp}")
            ln_and_router(sp)
            experts(sp, tail_cb=mk_tail(f"post{sp}"))
            store_x(sp, xf_r, "xf")
            post_a(sp)
        phase_b()
        for sp in range(NSP):
            load_x(sp, xf_r, "xf")
            add_ac(sp)
            stats(f"c{sp}")
            ln_and_router(sp)
            experts(sp)
            store_x(sp, out_r, "outd")
        fin = sched.op("sx:fin", lambda e: e.nop(),
                       reads=[f"outd.{sp}.{g}" for sp in range(NSP) for g in range(NGX)],
                       tag="final")
        fin.no_inc = True

        counts = sched.finalize()

        sems = {}
        for st in sched.ops:
            sems[st] = E_(nc.semaphore(f"sem_{st.replace(':', '_').replace('.', '_')}"))
        blk = E_(nc.Block())

        def emit(engname):
            def run(eng):
                for o in sched.eng_ops[engname]:
                    for (ws, wc) in o.waits:
                        mult = 16 if sched.is_dma(ws) else 1
                        eng.wait_ge(sems[ws], wc * mult)
                    inst = o.fn(eng)
                    if not o.no_inc:
                        mult = 16 if sched.is_dma(o.stream) else 1
                        inst.then_inc(sems[o.stream], mult)
            return run

        blk.tensor(emit("t"))
        blk.vector(emit("v"))
        blk.scalar(emit("s"))
        blk.gpsimd(emit("gx"))
        blk.sync(emit("sx"))
    return nc


def host_layout(inputs, cfg):
    KC, JC, JB, OB, NE, D, T = cfg.KC, cfg.JC, cfg.JB, cfg.OB, cfg.NE, cfg.D, cfg.T
    F2 = 2 * cfg.F
    x = np.asarray(inputs["L0_attn_x"], dtype=np.float32)
    w1 = np.asarray(inputs["L0_ffn_w1"], dtype=np.float32)
    w2 = np.asarray(inputs["L0_ffn_w2"], dtype=np.float32)
    wg = np.asarray(inputs["L0_ffn_wg"], dtype=np.float32)
    Wv = np.asarray(inputs["L0_attn_Wv"], dtype=np.float32)
    Wo = np.asarray(inputs["L0_attn_Wo"], dtype=np.float32)
    bv = np.asarray(inputs["L0_attn_bv"], dtype=np.float32)
    bo = np.asarray(inputs["L0_attn_bo"], dtype=np.float32)

    w1r = np.ascontiguousarray(
        w1.reshape(NE, KC, 128, JB, 128).transpose(0, 3, 2, 1, 4)).reshape(NE, JB, 128, KC * 128)
    w2r = np.ascontiguousarray(
        w2.reshape(NE, JC, 128, OB, 128).transpose(0, 3, 2, 1, 4)).reshape(NE, OB, 128, JC * 128)
    wgt = np.ascontiguousarray(wg.T.reshape(KC, 128, NE).transpose(1, 0, 2)).reshape(128, KC * NE)
    wvt = np.ascontiguousarray(
        Wv.T.reshape(KC, 128, OB, 128).transpose(2, 1, 0, 3)).reshape(OB, 128, KC * 128)
    wot = np.ascontiguousarray(
        Wo.T.reshape(KC, 128, OB, 128).transpose(2, 1, 0, 3)).reshape(OB, 128, KC * 128)
    bvt = np.ascontiguousarray(bv.reshape(KC, 128).T)
    bot = np.ascontiguousarray(bo.reshape(KC, 128).T)

    sel = np.zeros((8, NE * 128), np.float32)
    for e in range(NE):
        sel[e, e * 128:(e + 1) * 128] = 0.5
    idn = np.eye(128, dtype=np.float32)
    cst = np.zeros((128, 8), np.float32)
    cst[:, C_EPS] = 1e-5
    cst[:, C_NHALF] = -0.5
    cst[:, C_RD] = 1.0 / D
    cst[:, C_RT] = 1.0 / T
    cst[:, C_ONE] = 1.0
    onesr = np.ones((1, 128), np.float32)
    onesk = np.ones((128, 1), np.float32)

    common = {"w1r": w1r, "w2r": w2r, "wgt": wgt, "wvt": wvt, "wot": wot,
              "bvt": bvt, "bot": bot, "sel": sel, "idn": idn, "cst": cst,
              "onesr": onesr, "onesk": onesk}
    in_maps = []
    for c in range(x.shape[0]):
        m = dict(common)
        m["xt"] = np.ascontiguousarray(x[c].T)
        in_maps.append(m)
    return in_maps


def kernel(**inputs):
    cfg = CFG_REAL
    in_maps = host_layout(inputs, cfg)
    nc = build_program(cfg)
    res = run_bass_kernel_spmd(nc, in_maps, core_ids=list(range(8)))
    outs = [np.asarray(r["out"]) for r in res.results]
    full = np.stack(outs, axis=0).transpose(0, 2, 1)
    return np.ascontiguousarray(full.astype(np.float32))


if __name__ == "__main__":
    cfg = CFG_REAL
    import time
    t0 = time.time()
    nc = build_program(cfg)
    try:
        n = nc.n_instructions()
    except Exception:
        n = "?"
    print("build s:", time.time() - t0, "n_inst:", n)


# version 09
# speedup vs baseline: 1.0536x; 1.0152x over previous
import sys
if "/opt/trn_rl_repo" not in sys.path:
    sys.path.insert(0, "/opt/trn_rl_repo")
import numpy as np
import concourse.bass as bass
import concourse.mybir as mybir
from concourse.bass_utils import run_bass_kernel_spmd

F32 = mybir.dt.float32
BF16 = mybir.dt.bfloat16
AF = mybir.ActivationFunctionType
ALU = mybir.AluOpType
AX = mybir.AxisListType


class Cfg:
    def __init__(self, KC, JC, S, NSP):
        self.KC = KC
        self.JC = JC
        self.JB = 2 * JC
        self.OB = KC
        self.NE = 8
        self.S = S
        self.NSP = NSP
        self.T = S * NSP
        self.D = 128 * KC
        self.F = 128 * JC
        self.NT = min(512, S)
        self.GX = min(4, KC)
        self.NGX = KC // self.GX
        self.NTL = S // self.NT
        self.TT = S // 128


CFG_REAL = Cfg(KC=16, JC=11, S=1024, NSP=4)

C_EPS = 0
C_NHALF = 1
C_RD = 2
C_RT = 3
C_ONE = 4


class Op:
    __slots__ = ("stream", "engine", "fn", "idx", "waits", "tag", "no_inc")

    def __init__(self, stream, engine, fn, idx, tag):
        self.stream = stream
        self.engine = engine
        self.fn = fn
        self.idx = idx
        self.waits = []
        self.tag = tag
        self.no_inc = False


class Sched:
    ENGINES = ("t", "v", "s", "gx", "sx")
    DMA_ENG = frozenset(("gx", "sx"))

    def __init__(self):
        self.ops = {}
        self.eng_ops = {e: [] for e in self.ENGINES}
        self.order = []
        self.last_writer = {}
        self.readers = {}
        self.vc = {}
        self.engine_vc = {e: {} for e in self.ENGINES}
        self.snap = {}

    def is_dma(self, stream):
        return stream.split(":")[0] in self.DMA_ENG

    def op(self, stream, fn, reads=(), writes=(), tag=""):
        engine = stream.split(":")[0]
        lst = self.ops.setdefault(stream, [])
        o = Op(stream, engine, fn, len(lst), tag)
        lst.append(o)
        self.eng_ops[engine].append(o)
        self.order.append(o)
        deps = set()
        for b in reads:
            w = self.last_writer.get(b)
            if w is not None:
                deps.add(w)
            self.readers.setdefault(b, []).append(o)
        for b in writes:
            w = self.last_writer.get(b)
            if w is not None:
                deps.add(w)
            for r in self.readers.get(b, ()):
                deps.add(r)
            self.readers[b] = []
            self.last_writer[b] = o
        deps.discard(o)
        V = self.engine_vc[engine]
        for d in sorted(deps, key=lambda d: (d.stream, d.idx)):
            if d.engine == "t" and engine == "t":
                continue
            if d.engine == engine and d.stream == stream and not self.is_dma(stream):
                pass
            need = d.idx + 1
            if V.get(d.stream, 0) >= need:
                continue
            o.waits.append((d.stream, need))
            for k, v in self.snap[d].items():
                if V.get(k, 0) < v:
                    V[k] = v
            if V.get(d.stream, 0) < need:
                V[d.stream] = need
        s = dict(V)
        s[stream] = o.idx + 1
        self.snap[o] = s
        return o

    def finalize(self):
        for o in self.order:
            for (ws, wc) in o.waits:
                assert wc <= len(self.ops[ws]), (o.tag, ws, wc)
        return {st: len(v) for st, v in self.ops.items()}


def build_program(cfg):
    KC, JC, JB, OB, NE = cfg.KC, cfg.JC, cfg.JB, cfg.OB, cfg.NE
    S, NSP, T, D = cfg.S, cfg.NSP, cfg.T, cfg.D
    NT, NTL, TT = cfg.NT, cfg.NTL, cfg.TT
    EXOFF = max(S, 512)

    nc = bass.Bass("TRN2", target_bir_lowering=False)
    xt_d = nc.dram_tensor("xt", [D, T], F32, kind="ExternalInput").ap()
    w1_d = nc.dram_tensor("w1r", [NE, JB, 128, KC * 128], F32, kind="ExternalInput").ap()
    w2_d = nc.dram_tensor("w2r", [NE, OB, 128, JC * 128], F32, kind="ExternalInput").ap()
    wgt_d = nc.dram_tensor("wgt", [128, KC * 8], F32, kind="ExternalInput").ap()
    wvt_d = nc.dram_tensor("wvt", [OB, 128, KC * 128], F32, kind="ExternalInput").ap()
    wot_d = nc.dram_tensor("wot", [OB, 128, KC * 128], F32, kind="ExternalInput").ap()
    bvt_d = nc.dram_tensor("bvt", [128, KC], F32, kind="ExternalInput").ap()
    bot_d = nc.dram_tensor("bot", [128, KC], F32, kind="ExternalInput").ap()
    sel_d = nc.dram_tensor("sel", [8, NE * 128], F32, kind="ExternalInput").ap()
    idn_d = nc.dram_tensor("idn", [128, 128], F32, kind="ExternalInput").ap()
    cst_d = nc.dram_tensor("cst", [128, 8], F32, kind="ExternalInput").ap()
    onesr_d = nc.dram_tensor("onesr", [1, 128], F32, kind="ExternalInput").ap()
    onesk_d = nc.dram_tensor("onesk", [128, 1], F32, kind="ExternalInput").ap()
    out_d = nc.dram_tensor("out", [D, T], F32, kind="ExternalOutput").ap()
    xf_d = nc.dram_tensor("xf", [D, T], F32, kind="Internal").ap()
    w1c_d = nc.dram_tensor("w1c", [NE, JB, 128, KC * 128], BF16, kind="Internal").ap()
    w2c_d = nc.dram_tensor("w2c", [NE, OB, 128, JC * 128], BF16, kind="Internal").ap()

    xt_r = xt_d.rearrange("(c p) t -> p c t", p=128)
    xf_r = xf_d.rearrange("(c p) t -> p c t", p=128)
    out_r = out_d.rearrange("(c p) t -> p c t", p=128)

    from contextlib import ExitStack
    with ExitStack() as ctx:
        E_ = ctx.enter_context
        X = E_(nc.sbuf_tensor("X", [128, KC * S], F32))
        XLN = E_(nc.sbuf_tensor("XLN", [128, KC * S], BF16))
        G = E_(nc.sbuf_tensor("G", [128, JC * S], BF16))
        SQ = E_(nc.sbuf_tensor("SQ", [128, 4 * S], BF16))
        SILU = E_(nc.sbuf_tensor("SILU", [128, S], F32))
        TMP = E_(nc.sbuf_tensor("TMP", [128, S], F32))
        CH = E_(nc.sbuf_tensor("CH", [128, S], F32))
        W1B = E_(nc.sbuf_tensor("W1B", [128, 2 * KC * 128], BF16))
        W2B = E_(nc.sbuf_tensor("W2B", [128, 2 * JC * 128], BF16))
        LG = E_(nc.sbuf_tensor("LG", [8, S], F32))
        CS = E_(nc.sbuf_tensor("CS", [8, S], F32))
        R_MU = E_(nc.sbuf_tensor("R_MU", [1, S], F32))
        R_V = E_(nc.sbuf_tensor("R_V", [1, S], F32))
        R_MR = E_(nc.sbuf_tensor("R_MR", [1, S], F32))
        R_T4 = E_(nc.sbuf_tensor("R_T4", [1, S], F32))
        RT = E_(nc.sbuf_tensor("RT", [128, 8 * TT], F32))
        DD = E_(nc.sbuf_tensor("DD", [128, 8 * TT], F32))
        SELS = E_(nc.sbuf_tensor("SELS", [128, 8 * TT], F32))
        EE = E_(nc.sbuf_tensor("EE", [128, 8 * TT], F32))
        CTT = E_(nc.sbuf_tensor("CTT", [128, 8 * TT], F32))
        MRB = E_(nc.sbuf_tensor("MRB", [128, 8], F32))
        COLS = E_(nc.sbuf_tensor("COLS", [128, 4], F32))
        DL2S = E_(nc.sbuf_tensor("DL2S", [128, TT], F32))
        E2 = E_(nc.sbuf_tensor("E2", [128, TT], F32))
        DEN = E_(nc.sbuf_tensor("DEN", [128, TT], F32))
        RDEN = E_(nc.sbuf_tensor("RDEN", [128, TT], F32))
        A1 = E_(nc.sbuf_tensor("A1", [128, KC], F32))
        A1SP = E_(nc.sbuf_tensor("A1SP", [128, NSP * KC], F32))
        SMR = E_(nc.sbuf_tensor("SMR", [1, 2], F32))
        SMRC = E_(nc.sbuf_tensor("SMRC", [128, 1], F32))
        XBAR = E_(nc.sbuf_tensor("XBAR", [128, KC], F32))
        VC = E_(nc.sbuf_tensor("VC", [128, KC], F32))
        AC = E_(nc.sbuf_tensor("AC", [128, KC], F32))
        BVT = E_(nc.sbuf_tensor("BVT", [128, KC], F32))
        BOT = E_(nc.sbuf_tensor("BOT", [128, KC], F32))
        SLAB = E_(nc.sbuf_tensor("SLAB", [128, 2 * KC * 128], F32))
        CST = E_(nc.sbuf_tensor("CST", [128, 8], F32))
        IDN = E_(nc.sbuf_tensor("IDN", [128, 128], F32))
        SEL = E_(nc.sbuf_tensor("SEL", [8, NE * 128], F32))
        WGTB = E_(nc.sbuf_tensor("WGTB", [128, KC * 8], BF16))
        ONESR = E_(nc.sbuf_tensor("ONESR", [1, 128], F32))
        ONESK = E_(nc.sbuf_tensor("ONESK", [128, 1], F32))
        ONESKB = E_(nc.sbuf_tensor("ONESKB", [128, 1], BF16))
        pA = E_(nc.psum_tensor("pA", [128, max(2 * S, EXOFF + max(S, 512))], F32))
        pB = E_(nc.psum_tensor("pB", [128, 2 * S], F32))

        sched = Sched()

        def pa_keys(lo, hi):
            return [f"pA.{b}" for b in range(lo // 512, (hi - 1) // 512 + 1)]

        def pb_keys(lo, hi):
            return [f"pB.{b}" for b in range(lo // 512, (hi - 1) // 512 + 1)]

        def const_loads():
            pairs = [(CST, cst_d, "CST"), (IDN, idn_d, "IDN"), (SEL, sel_d, "SEL"),
                     (ONESR, onesr_d, "ONESR"),
                     (ONESK, onesk_d, "ONESK"), (BVT, bvt_d, "BVT"), (BOT, bot_d, "BOT")]
            for tb, src, key in pairs:
                sched.op(f"sx:c.{key}", lambda e, tb=tb, src=src: e.dma_start(out=tb[:, :], in_=src),
                         writes=[key], tag=f"ld_{key}")
            sched.op("gx:c.WGTB", lambda e: e.dma_start(out=WGTB[:, :], in_=wgt_d),
                     writes=["WGTB"], tag="ld_WGTB")
            sched.op("gx:c.ONESKB", lambda e: e.dma_start(out=ONESKB[:, :], in_=onesk_d),
                     writes=["ONESKB"], tag="ld_ONESKB")
            sched.op("v", lambda e: e.memset(SMR[:, :], 0.0), writes=["SMR"], tag="z_SMR")

        GXC, NGX = cfg.GX, cfg.NGX

        def load_x(sp, src_r, srckey):
            for g in range(NGX):
                def ld(e, g=g, sp=sp, src_r=src_r):
                    return e.dma_start(
                        out=X.ap().rearrange("p (c t) -> p c t", t=S)[:, GXC * g:GXC * (g + 1), :],
                        in_=src_r[:, GXC * g:GXC * (g + 1), sp * S:(sp + 1) * S])
                rd = [] if srckey is None else [f"{srckey}.{sp}.{g}"]
                sched.op(f"sx:x{g}", ld, reads=rd,
                         writes=[f"X.{c}" for c in range(GXC * g, GXC * (g + 1))], tag=f"ldx{sp}.{g}")

        def add_ac(sp):
            for c in range(KC):
                sched.op("v", lambda e, c=c: e.tensor_scalar(
                    X[:, c * S:(c + 1) * S], X[:, c * S:(c + 1) * S],
                    AC[:, c:c + 1], None, ALU.add),
                    reads=[f"X.{c}", "AC"], writes=[f"X.{c}"], tag=f"addac{c}")

        def stats_sq(tag, c, eng="v"):
            p = c % 4
            if eng == "s":
                sched.op("s", lambda e, c=c, p=p: e.activation(
                    SQ[:, p * S:(p + 1) * S], X[:, c * S:(c + 1) * S], AF.Square),
                    reads=[f"X.{c}"], writes=[f"SQ.{p}"], tag=f"{tag}_sq{c}")
            else:
                sched.op("v", lambda e, c=c, p=p: e.tensor_tensor(
                    SQ[:, p * S:(p + 1) * S], X[:, c * S:(c + 1) * S],
                    X[:, c * S:(c + 1) * S], ALU.mult),
                    reads=[f"X.{c}"], writes=[f"SQ.{p}"], tag=f"{tag}_sq{c}")

        def stats_xsum(tag, c):
            def smm(e, c=c):
                for n in range(NTL):
                    i = e.matmul(pA[0:1, n * NT:(n + 1) * NT], ONESK[:, :],
                                 X[:, c * S + n * NT:c * S + (n + 1) * NT],
                                 start=(c == 0), stop=(c == KC - 1))
                return i
            sched.op("t", smm, reads=[f"X.{c}", "ONESK"],
                     writes=pa_keys(0, S), tag=f"{tag}_xmm{c}")

        def stats_ex2(tag, c):
            p = c % 4
            def smm(e, c=c, p=p):
                for n in range(NTL):
                    i = e.matmul(pA[0:1, EXOFF + n * NT:EXOFF + (n + 1) * NT], ONESKB[:, :],
                                 SQ[:, p * S + n * NT:p * S + (n + 1) * NT],
                                 start=(c == 0), stop=(c == KC - 1))
                return i
            sched.op("t", smm, reads=[f"SQ.{p}", "ONESKB"],
                     writes=pa_keys(EXOFF, EXOFF + S), tag=f"{tag}_emm{c}")

        def warm(tag, i, rowt, rowkey):
            sched.op("t", lambda e, rowt=rowt: e.matmul(
                pB[:, S:S + 64], ONESR[:, :], rowt[0:1, 0:64], start=True, stop=True),
                reads=[rowkey, "ONESR"], writes=pb_keys(S, S + 64), tag=f"{tag}_wm{i}")

        def stats_rows(tag, mr):
            sched.op("v", lambda e: e.tensor_scalar(R_MU[:, :], pA[0:1, 0:S],
                     CST[0:1, C_RD:C_RD + 1], None, ALU.mult),
                     reads=pa_keys(0, S) + ["CST"], writes=["R_MU"], tag=f"{tag}_rmu")
            warm(tag, 0, R_MU, "R_MU")
            sched.op("v", lambda e: e.tensor_tensor(R_T4[:, :], R_MU[:, :], R_MU[:, :], ALU.mult),
                     reads=["R_MU"], writes=["R_T4"], tag=f"{tag}_rmu2")
            warm(tag, 1, R_T4, "R_T4")
            sched.op("v", lambda e: e.scalar_tensor_tensor(
                R_V[:, :], pA[0:1, EXOFF:EXOFF + S], CST[0:1, C_RD:C_RD + 1], R_T4[:, :],
                ALU.mult, ALU.subtract),
                reads=pa_keys(EXOFF, EXOFF + S) + ["CST", "R_T4"], writes=["R_V"], tag=f"{tag}_rvar")
            warm(tag, 2, R_V, "R_V")
            sched.op("s", lambda e: e.activation(R_V[:, :], R_V[:, :], AF.Ln,
                     bias=CST[0:1, C_EPS:C_EPS + 1]),
                     reads=["R_V", "CST"], writes=["R_V"], tag=f"{tag}_rln")
            warm(tag, 3, R_V, "R_V")
            sched.op("s", lambda e: e.activation(R_V[:, :], R_V[:, :], AF.Exp,
                     scale=CST[0:1, C_NHALF:C_NHALF + 1]),
                     reads=["R_V", "CST"], writes=["R_V"], tag=f"{tag}_rexp")
            warm(tag, 4, R_V, "R_V")
            if mr:
                sched.op("v", lambda e: e.tensor_tensor(R_MR[:, :], R_MU[:, :], R_V[:, :], ALU.mult),
                         reads=["R_MU", "R_V"], writes=["R_MR"], tag=f"{tag}_rmr")

        def stats(tag, mr=False):
            for c in range(KC):
                stats_sq(tag, c, eng=("s" if c % 2 == 0 else "v"))
                stats_xsum(tag, c)
                stats_ex2(tag, c)
            stats_rows(tag, mr)

        def ln_and_router(sp):
            def bc_mu(e):
                for n in range(NTL):
                    i = e.matmul(pB[:, n * NT:(n + 1) * NT], ONESR[:, :],
                                 R_MU[:, n * NT:(n + 1) * NT], start=True, stop=True)
                return i
            sched.op("t", bc_mu, reads=["R_MU", "ONESR"], writes=pb_keys(0, S), tag="lnbcmu")

            def bc_rs(e):
                for n in range(NTL):
                    i = e.matmul(pB[:, S + n * NT:S + (n + 1) * NT], ONESR[:, :],
                                 R_V[:, n * NT:(n + 1) * NT], start=True, stop=True)
                return i
            sched.op("t", bc_rs, reads=["R_V", "ONESR"], writes=pb_keys(S, 2 * S), tag="lnbcrs")
            for c in range(KC):
                sched.op("v", lambda e, c=c: e.tensor_tensor(
                    TMP[:, :], X[:, c * S:(c + 1) * S], pB[:, 0:S], ALU.subtract),
                    reads=[f"X.{c}"] + pb_keys(0, S), writes=["TMP"], tag=f"lna{c}")
                sched.op("v", lambda e, c=c: e.tensor_tensor(
                    XLN[:, c * S:(c + 1) * S], TMP[:, :], pB[:, S:2 * S], ALU.mult),
                    reads=["TMP"] + pb_keys(S, 2 * S), writes=[f"XLN.{c}"], tag=f"lnb{c}")
            for c in range(KC):
                def rmm(e, c=c):
                    for n in range(NTL):
                        i = e.matmul(pA[0:8, n * NT:(n + 1) * NT], WGTB[:, c * 8:(c + 1) * 8],
                                     XLN[:, c * S + n * NT:c * S + (n + 1) * NT],
                                     start=(c == 0), stop=(c == KC - 1))
                    return i
                sched.op("t", rmm, reads=[f"XLN.{c}", "WGTB"], writes=pa_keys(0, S), tag=f"rmm{c}")
            sched.op("s", lambda e: e.copy(LG[:, :], pA[0:8, 0:S]),
                     reads=pa_keys(0, S), writes=["LG"], tag="lgcp")
            def tp(e):
                for t_ in range(TT):
                    i = e.transpose(pA[:, S + 8 * t_:S + 8 * t_ + 8],
                                    LG[0:8, t_ * 128:(t_ + 1) * 128], IDN[0:8, 0:8])
                return i
            sched.op("t", tp, reads=["LG", "IDN"], writes=pa_keys(S, S + 8 * TT), tag="lgtp")
            sched.op("v", lambda e: e.tensor_copy(RT[:, :], pA[:, S:S + 8 * TT]),
                     reads=pa_keys(S, S + 8 * TT), writes=["RT"], tag="rtcp")
            for t_ in range(TT):
                rt = RT[:, 8 * t_:8 * t_ + 8]
                sched.op("v", lambda e, rt=rt: e.max(out=MRB[:, :], in_=rt),
                         reads=["RT"], writes=["MRB"], tag=f"m8_{t_}")
                sched.op("v", lambda e, rt=rt, t_=t_: e.tensor_scalar(
                    DD[:, 8 * t_:8 * t_ + 8], rt, MRB[:, 0:1], None, ALU.subtract),
                    reads=["RT", "MRB"], writes=["DD"], tag=f"dd_{t_}")
                sched.op("v", lambda e, t_=t_: e.tensor_tensor(
                    DL2S[:, t_:t_ + 1], MRB[:, 1:2], MRB[:, 0:1], ALU.subtract),
                    reads=["MRB"], writes=["DL2S"], tag=f"dl2_{t_}")
                sched.op("v", lambda e, t_=t_: e.tensor_scalar(
                    SELS[:, 8 * t_:8 * t_ + 8], DD[:, 8 * t_:8 * t_ + 8],
                    DL2S[:, t_:t_ + 1], None, ALU.is_ge),
                    reads=["DD", "DL2S"], writes=["SELS"], tag=f"sel_{t_}")
            sched.op("s", lambda e: e.activation(EE[:, :], DD[:, :], AF.Exp),
                     reads=["DD"], writes=["EE"], tag="ee")
            sched.op("s", lambda e: e.activation(E2[:, :], DL2S[:, :], AF.Exp),
                     reads=["DL2S"], writes=["E2"], tag="e2")
            sched.op("v", lambda e: e.tensor_scalar(DEN[:, :], E2[:, :],
                     CST[:, C_ONE:C_ONE + 1], None, ALU.add),
                     reads=["E2", "CST"], writes=["DEN"], tag="den")
            sched.op("v", lambda e: e.reciprocal(RDEN[:, :], DEN[:, :]),
                     reads=["DEN"], writes=["RDEN"], tag="rden")
            for t_ in range(TT):
                sched.op("v", lambda e, t_=t_: e.tensor_scalar(
                    CTT[:, 8 * t_:8 * t_ + 8], EE[:, 8 * t_:8 * t_ + 8],
                    RDEN[:, t_:t_ + 1], None, ALU.mult),
                    reads=["EE", "RDEN"], writes=["CTT"], tag=f"ctw_{t_}")
                sched.op("v", lambda e, t_=t_: e.tensor_tensor(
                    CTT[:, 8 * t_:8 * t_ + 8], CTT[:, 8 * t_:8 * t_ + 8],
                    SELS[:, 8 * t_:8 * t_ + 8], ALU.mult),
                    reads=["CTT", "SELS"], writes=["CTT"], tag=f"cts_{t_}")
            def tpb(e):
                for t_ in range(TT):
                    i = e.transpose(pA[0:8, t_ * 128:(t_ + 1) * 128],
                                    CTT[:, 8 * t_:8 * t_ + 8], IDN[:, :])
                return i
            sched.op("t", tpb, reads=["CTT", "IDN"], writes=pa_keys(0, S), tag="cttp")
            sched.op("s", lambda e: e.copy(CS[:, :], pA[0:8, 0:S]),
                     reads=pa_keys(0, S), writes=["CS"], tag="cscp")

        wq_ctr = [0]
        wo_ctr = [0]
        first_pass = [True]

        def experts(sp, tail_cb=None):
            cast_pass = first_pass[0]
            first_pass[0] = False
            for ei in range(NE):
                def emit_ch(ei=ei):
                    def chmm(e, ei=ei):
                        for n in range(NTL):
                            i = e.matmul(pB[:, n * NT:(n + 1) * NT],
                                         SEL[0:8, ei * 128:(ei + 1) * 128],
                                         CS[0:8, n * NT:(n + 1) * NT], start=True, stop=True)
                        return i
                    sched.op("t", chmm, reads=["CS", "SEL"], writes=pb_keys(0, S), tag=f"chmm{ei}")
                    sched.op("s", lambda e: e.copy(CH[:, :], pB[:, 0:S]),
                             reads=pb_keys(0, S), writes=["CH"], tag=f"chcp{ei}")
                for j in range(JC):
                    for half, q in ((0, j), (1, j + JC)):
                        par = wq_ctr[0] % 2
                        wq_ctr[0] += 1
                        if cast_pass:
                            sched.op(f"gx:w1.{par}", lambda e, ei=ei, q=q, par=par: e.dma_start(
                                out=W1B[:, par * KC * 128:(par + 1) * KC * 128], in_=w1_d[ei, q]),
                                writes=[f"W1B.{par}"], tag=f"w1d{ei}.{q}")
                            sched.op(f"gx:wb1.{par}", lambda e, ei=ei, q=q, par=par: e.dma_start(
                                out=w1c_d[ei, q],
                                in_=W1B[:, par * KC * 128:(par + 1) * KC * 128]),
                                reads=[f"W1B.{par}"], writes=[f"w1c.{ei}.{q}"], tag=f"w1wb{ei}.{q}")
                        else:
                            sched.op(f"gx:w1.{par}", lambda e, ei=ei, q=q, par=par: e.dma_start(
                                out=W1B[:, par * KC * 128:(par + 1) * KC * 128], in_=w1c_d[ei, q]),
                                reads=[f"w1c.{ei}.{q}"], writes=[f"W1B.{par}"], tag=f"w1d{ei}.{q}")

                        def m1(e, par=par, half=half):
                            for kc in range(KC):
                                for n in range(NTL):
                                    i = e.matmul(
                                        pA[:, half * S + n * NT:half * S + (n + 1) * NT],
                                        W1B[:, par * KC * 128 + kc * 128:par * KC * 128 + (kc + 1) * 128],
                                        XLN[:, kc * S + n * NT:kc * S + (n + 1) * NT],
                                        start=(kc == 0), stop=(kc == KC - 1))
                            return i
                        sched.op("t", m1, reads=[f"W1B.{par}"] + [f"XLN.{c}" for c in range(KC)],
                                 writes=pa_keys(half * S, (half + 1) * S), tag=f"m1_{ei}.{q}")
                    if j == 0:
                        emit_ch()
                    sched.op("s", lambda e: e.activation(SILU[:, :], pA[:, 0:S], AF.Silu),
                             reads=pa_keys(0, S), writes=["SILU"], tag=f"silu{ei}.{j}")
                    sched.op("v", lambda e: e.tensor_tensor(TMP[:, :], SILU[:, :], CH[:, :], ALU.mult),
                             reads=["SILU", "CH"], writes=["TMP"], tag=f"t1_{ei}.{j}")
                    sched.op("v", lambda e, j=j: e.tensor_tensor(
                        G[:, j * S:(j + 1) * S], TMP[:, :], pA[:, S:2 * S], ALU.mult),
                        reads=["TMP"] + pa_keys(S, 2 * S), writes=[f"G.{j}"], tag=f"g_{ei}.{j}")
                for ob in range(OB):
                    par = wo_ctr[0] % 2
                    wo_ctr[0] += 1
                    if cast_pass:
                        sched.op(f"gx:w2.{par}", lambda e, ei=ei, ob=ob, par=par: e.dma_start(
                            out=W2B[:, par * JC * 128:(par + 1) * JC * 128], in_=w2_d[ei, ob]),
                            writes=[f"W2B.{par}"], tag=f"w2d{ei}.{ob}")
                        sched.op(f"gx:wb2.{par}", lambda e, ei=ei, ob=ob, par=par: e.dma_start(
                            out=w2c_d[ei, ob],
                            in_=W2B[:, par * JC * 128:(par + 1) * JC * 128]),
                            reads=[f"W2B.{par}"], writes=[f"w2c.{ei}.{ob}"], tag=f"w2wb{ei}.{ob}")
                    else:
                        sched.op(f"gx:w2.{par}", lambda e, ei=ei, ob=ob, par=par: e.dma_start(
                            out=W2B[:, par * JC * 128:(par + 1) * JC * 128], in_=w2c_d[ei, ob]),
                            reads=[f"w2c.{ei}.{ob}"], writes=[f"W2B.{par}"], tag=f"w2d{ei}.{ob}")

                    def m2(e, ob=ob, par=par):
                        for jc in range(JC):
                            for n in range(NTL):
                                i = e.matmul(
                                    pB[:, (ob % 2) * S + n * NT:(ob % 2) * S + (n + 1) * NT],
                                    W2B[:, par * JC * 128 + jc * 128:par * JC * 128 + (jc + 1) * 128],
                                    G[:, jc * S + n * NT:jc * S + (n + 1) * NT],
                                    start=(jc == 0), stop=(jc == JC - 1))
                        return i
                    sched.op("t", m2, reads=[f"W2B.{par}"] + [f"G.{j}" for j in range(JC)],
                             writes=pb_keys((ob % 2) * S, (ob % 2 + 1) * S), tag=f"m2_{ei}.{ob}")
                    sched.op("v", lambda e, ob=ob: e.tensor_tensor(
                        X[:, ob * S:(ob + 1) * S], X[:, ob * S:(ob + 1) * S],
                        pB[:, (ob % 2) * S:(ob % 2) * S + S], ALU.add),
                        reads=[f"X.{ob}"] + pb_keys((ob % 2) * S, (ob % 2 + 1) * S),
                        writes=[f"X.{ob}"], tag=f"ev_{ei}.{ob}")
                    if tail_cb is not None and ei == NE - 1:
                        tail_cb(ob)

        def mk_tail(tag):
            def cb(ob):
                stats_sq(tag, ob)
            return cb

        def store_x(sp, dst_r, dstkey):
            for g in range(NGX):
                def st(e, g=g, sp=sp, dst_r=dst_r):
                    return e.dma_start(
                        out=dst_r[:, GXC * g:GXC * (g + 1), sp * S:(sp + 1) * S],
                        in_=X.ap().rearrange("p (c t) -> p c t", t=S)[:, GXC * g:GXC * (g + 1), :])
                sched.op(f"sx:st{g}", st, reads=[f"X.{c}" for c in range(GXC * g, GXC * (g + 1))],
                         writes=[f"{dstkey}.{sp}.{g}"], tag=f"st{dstkey}{sp}.{g}")

        def post_a(sp):
            for c in range(KC):
                stats_xsum(f"post{sp}", c)
                stats_ex2(f"post{sp}", c)
            stats_rows(f"post{sp}", mr=True)
            sched.op("v", lambda e: e.tensor_scalar(R_T4[:, :], R_V[:, :],
                     CST[0:1, C_RT:C_RT + 1], None, ALU.mult),
                     reads=["R_V", "CST"], writes=["R_T4"], tag="t4")
            sched.op("v", lambda e: e.reduce_sum(SMR[0:1, 1:2], R_MR[:, :], axis=AX.X),
                     reads=["R_MR"], writes=["SMR"], tag="smr1")
            sched.op("v", lambda e: e.tensor_scalar(SMR[0:1, 1:2], SMR[0:1, 1:2],
                     CST[0:1, C_RT:C_RT + 1], None, ALU.mult),
                     reads=["SMR", "CST"], writes=["SMR"], tag="smr2")
            sched.op("v", lambda e: e.tensor_tensor(SMR[0:1, 0:1], SMR[0:1, 0:1],
                     SMR[0:1, 1:2], ALU.add),
                     reads=["SMR"], writes=["SMR"], tag="smr3")
            def bct4(e):
                for n in range(NTL):
                    i = e.matmul(pB[:, n * NT:(n + 1) * NT], ONESR[:, :],
                                 R_T4[:, n * NT:(n + 1) * NT], start=True, stop=True)
                return i
            sched.op("t", bct4, reads=["R_T4", "ONESR"], writes=pb_keys(0, S), tag="bct4")
            for c in range(KC):
                sched.op("v", lambda e, c=c, sp=sp: e.scalar_tensor_tensor(
                    TMP[:, :], X[:, c * S:(c + 1) * S], 1.0, pB[:, 0:S],
                    ALU.mult, ALU.mult,
                    accum_out=A1SP[:, sp * KC + c:sp * KC + c + 1]),
                    reads=[f"X.{c}"] + pb_keys(0, S), writes=["TMP", "A1SP"], tag=f"a1m{c}")

        def phase_b():
            sched.op("t", lambda e: e.matmul(pA[:, 0:1], ONESR[:, :], SMR[0:1, 0:1],
                     start=True, stop=True),
                     reads=["SMR", "ONESR"], writes=pa_keys(0, 1), tag="smrbc")
            sched.op("v", lambda e: e.tensor_copy(SMRC[:, :], pA[:, 0:1]),
                     reads=pa_keys(0, 1), writes=["SMRC"], tag="smrcp")
            sched.op("v", lambda e: e.tensor_tensor(A1[:, :], A1SP[:, 0:KC],
                     A1SP[:, KC:2 * KC], ALU.add),
                     reads=["A1SP"], writes=["A1"], tag="a1s0")
            for spp in range(2, NSP):
                sched.op("v", lambda e, spp=spp: e.tensor_tensor(A1[:, :], A1[:, :],
                         A1SP[:, spp * KC:(spp + 1) * KC], ALU.add),
                         reads=["A1", "A1SP"], writes=["A1"], tag=f"a1s{spp}")
            sched.op("v", lambda e: e.tensor_scalar(XBAR[:, :], A1[:, :], SMRC[:, :],
                     None, ALU.subtract),
                     reads=["A1", "SMRC"], writes=["XBAR"], tag="xbar")

            def matvec(wsrc, rhs, rhskey, dst, dstkey, bias, biaskey, tag):
                for oc in range(OB):
                    par = oc % 2
                    sched.op(f"sx:sl{par}", lambda e, oc=oc, par=par, wsrc=wsrc: e.dma_start(
                        out=SLAB[:, par * KC * 128:(par + 1) * KC * 128], in_=wsrc[oc]),
                        writes=[f"SLAB.{par}"], tag=f"{tag}_sl{oc}")
                    pcol = par * min(512, S)

                    def mv(e, par=par, pcol=pcol, rhs=rhs):
                        for kc in range(KC):
                            i = e.matmul(pA[:, pcol:pcol + 1],
                                         SLAB[:, par * KC * 128 + kc * 128:par * KC * 128 + (kc + 1) * 128],
                                         rhs[:, kc:kc + 1],
                                         start=(kc == 0), stop=(kc == KC - 1))
                        return i
                    sched.op("t", mv, reads=[f"SLAB.{par}", rhskey],
                             writes=pa_keys(pcol, pcol + 1), tag=f"{tag}_mv{oc}")
                    sched.op("v", lambda e, oc=oc, pcol=pcol, dst=dst, bias=bias: e.tensor_tensor(
                        dst[:, oc:oc + 1], pA[:, pcol:pcol + 1], bias[:, oc:oc + 1], ALU.add),
                        reads=pa_keys(pcol, pcol + 1) + [biaskey], writes=[dstkey],
                        tag=f"{tag}_ev{oc}")
            matvec(wvt_d, XBAR, "XBAR", VC, "VC", BVT, "BVT", "mvv")
            matvec(wot_d, VC, "VC", AC, "AC", BOT, "BOT", "mvo")

        const_loads()
        for sp in range(NSP):
            load_x(sp, xt_r, None)
            stats(f"pre{sp}")
            ln_and_router(sp)
            experts(sp, tail_cb=mk_tail(f"post{sp}"))
            store_x(sp, xf_r, "xf")
            post_a(sp)
        phase_b()
        for sp in range(NSP):
            load_x(sp, xf_r, "xf")
            add_ac(sp)
            stats(f"c{sp}")
            ln_and_router(sp)
            experts(sp)
            store_x(sp, out_r, "outd")
        fin = sched.op("sx:fin", lambda e: e.nop(),
                       reads=[f"outd.{sp}.{g}" for sp in range(NSP) for g in range(NGX)],
                       tag="final")
        fin.no_inc = True

        counts = sched.finalize()

        sems = {}
        for st in sched.ops:
            sems[st] = E_(nc.semaphore(f"sem_{st.replace(':', '_').replace('.', '_')}"))
        blk = E_(nc.Block())

        def emit(engname):
            def run(eng):
                for o in sched.eng_ops[engname]:
                    for (ws, wc) in o.waits:
                        mult = 16 if sched.is_dma(ws) else 1
                        eng.wait_ge(sems[ws], wc * mult)
                    inst = o.fn(eng)
                    if not o.no_inc:
                        mult = 16 if sched.is_dma(o.stream) else 1
                        inst.then_inc(sems[o.stream], mult)
            return run

        blk.tensor(emit("t"))
        blk.vector(emit("v"))
        blk.scalar(emit("s"))
        blk.gpsimd(emit("gx"))
        blk.sync(emit("sx"))
    return nc


def host_layout(inputs, cfg):
    KC, JC, JB, OB, NE, D, T = cfg.KC, cfg.JC, cfg.JB, cfg.OB, cfg.NE, cfg.D, cfg.T
    F2 = 2 * cfg.F
    x = np.asarray(inputs["L0_attn_x"], dtype=np.float32)
    w1 = np.asarray(inputs["L0_ffn_w1"], dtype=np.float32)
    w2 = np.asarray(inputs["L0_ffn_w2"], dtype=np.float32)
    wg = np.asarray(inputs["L0_ffn_wg"], dtype=np.float32)
    Wv = np.asarray(inputs["L0_attn_Wv"], dtype=np.float32)
    Wo = np.asarray(inputs["L0_attn_Wo"], dtype=np.float32)
    bv = np.asarray(inputs["L0_attn_bv"], dtype=np.float32)
    bo = np.asarray(inputs["L0_attn_bo"], dtype=np.float32)

    w1r = np.ascontiguousarray(
        w1.reshape(NE, KC, 128, JB, 128).transpose(0, 3, 2, 1, 4)).reshape(NE, JB, 128, KC * 128)
    w2r = np.ascontiguousarray(
        w2.reshape(NE, JC, 128, OB, 128).transpose(0, 3, 2, 1, 4)).reshape(NE, OB, 128, JC * 128)
    wgt = np.ascontiguousarray(wg.T.reshape(KC, 128, NE).transpose(1, 0, 2)).reshape(128, KC * NE)
    wvt = np.ascontiguousarray(
        Wv.T.reshape(KC, 128, OB, 128).transpose(2, 1, 0, 3)).reshape(OB, 128, KC * 128)
    wot = np.ascontiguousarray(
        Wo.T.reshape(KC, 128, OB, 128).transpose(2, 1, 0, 3)).reshape(OB, 128, KC * 128)
    bvt = np.ascontiguousarray(bv.reshape(KC, 128).T)
    bot = np.ascontiguousarray(bo.reshape(KC, 128).T)

    sel = np.zeros((8, NE * 128), np.float32)
    for e in range(NE):
        sel[e, e * 128:(e + 1) * 128] = 0.5
    idn = np.eye(128, dtype=np.float32)
    cst = np.zeros((128, 8), np.float32)
    cst[:, C_EPS] = 1e-5
    cst[:, C_NHALF] = -0.5
    cst[:, C_RD] = 1.0 / D
    cst[:, C_RT] = 1.0 / T
    cst[:, C_ONE] = 1.0
    onesr = np.ones((1, 128), np.float32)
    onesk = np.ones((128, 1), np.float32)

    common = {"w1r": w1r, "w2r": w2r, "wgt": wgt, "wvt": wvt, "wot": wot,
              "bvt": bvt, "bot": bot, "sel": sel, "idn": idn, "cst": cst,
              "onesr": onesr, "onesk": onesk}
    in_maps = []
    for c in range(x.shape[0]):
        m = dict(common)
        m["xt"] = np.ascontiguousarray(x[c].T)
        in_maps.append(m)
    return in_maps


def kernel(**inputs):
    cfg = CFG_REAL
    in_maps = host_layout(inputs, cfg)
    nc = build_program(cfg)
    res = run_bass_kernel_spmd(nc, in_maps, core_ids=list(range(8)))
    outs = [np.asarray(r["out"]) for r in res.results]
    full = np.stack(outs, axis=0).transpose(0, 2, 1)
    return np.ascontiguousarray(full.astype(np.float32))


if __name__ == "__main__":
    cfg = CFG_REAL
    import time
    t0 = time.time()
    nc = build_program(cfg)
    try:
        n = nc.n_instructions()
    except Exception:
        n = "?"
    print("build s:", time.time() - t0, "n_inst:", n)
